# version 04 (final)
# baseline (speedup 1.0000x reference)
import numpy as np
import concourse.bass as bass
import concourse.mybir as mybir
from concourse.bass_utils import run_bass_kernel_spmd

F32 = mybir.dt.float32
BF16 = mybir.dt.bfloat16
I32 = mybir.dt.int32
AF = mybir.ActivationFunctionType
OP = mybir.AluOpType
AX = mybir.AxisListType

B, L, D = 2, 8192, 1024
NCORE, GRP, SEG = 8, 4, 2048
NTT = SEG // 128
DI, NH = 2048, 16
CS, NCH = 256, SEG // 256
DHID = 4096

DEBUG = False
PHASE_LIMIT = 99


class E:

    def __init__(self, nc, eng, name):
        self.eng = eng
        self.name = name
        self.tick = nc.alloc_semaphore(f"tk_{name}")
        self.nt = 0
        self.dsem = nc.alloc_semaphore(f"dm_{name}")
        self.nd = 0
        self.last = None
        self._ticked = None
        self._water = {}

    def do(self, inst):
        self.last = inst
        return inst

    def token(self):
        if self.last is None:
            return None
        if self.last is not self._ticked:
            self.nt += 1
            self.last.then_inc(self.tick, 1)
            self._ticked = self.last
        return ("t", self, self.nt)

    def dma(self, out, in_, **kw):
        inst = self.eng.dma_start(out=out, in_=in_, **kw)
        self.last = inst
        self.nd += 16
        inst.then_inc(self.dsem, 16)
        self._ticked = inst
        return ("d", self, self.nd)

    def dtoken(self):
        return ("d", self, self.nd) if self.nd else None

    def fence(self):
        if self.last is None:
            return
        if self.last is not self._ticked:
            self.nt += 1
            self.last.then_inc(self.tick, 1)
            self._ticked = self.last
        self.eng.wait_ge(self.tick, self.nt)

    def wait(self, *tokens):
        for tok in tokens:
            if tok is None:
                continue
            if isinstance(tok, list):
                self.wait(*tok)
                continue
            kind, ep, v = tok
            if ep is self and kind == "t":
                continue
            key = (kind, ep.name)
            if self._water.get(key, -1) >= v:
                continue
            self._water[key] = v
            self.eng.wait_ge(ep.tick if kind == "t" else ep.dsem, v)


def stage_inputs(inputs):
    f32 = np.float32
    x = np.ascontiguousarray(inputs["L0_attn_x"], dtype=f32)
    shared = dict(
        w1t=np.ascontiguousarray(inputs["L0_ffn_w1"].T, dtype=f32),
        w2t=np.ascontiguousarray(inputs["L0_ffn_w2"].T, dtype=f32),
        cw0=np.ascontiguousarray(
            np.asarray(inputs["L0_ffn_conv_w"], dtype=f32)
            .reshape(32, 128, 3).transpose(1, 0, 2).reshape(128, 96)),
        cb0=np.ascontiguousarray(
            np.asarray(inputs["L0_ffn_conv_b"], dtype=f32).reshape(32, 128).T),
        nw=np.ascontiguousarray(
            np.asarray(inputs["L1_attn_norm_w"], dtype=f32)[None, :]),
        cw1=np.ascontiguousarray(
            np.asarray(inputs["L1_attn_conv_w"], dtype=f32)
            .reshape(16, 128, 4).transpose(1, 0, 2).reshape(128, 64)),
        cb1=np.ascontiguousarray(
            np.asarray(inputs["L1_attn_conv_b"], dtype=f32).reshape(16, 128).T),
        alog=np.ascontiguousarray(
            np.asarray(inputs["L1_attn_A_log"], dtype=f32)[:, None]),
        dtb=np.ascontiguousarray(
            np.asarray(inputs["L1_attn_dt_bias"], dtype=f32)[:, None]),
        dsk=np.ascontiguousarray(
            np.asarray(inputs["L1_attn_D_skip"], dtype=f32).reshape(16, 128).T),
        wot=np.ascontiguousarray(inputs["L1_attn_w_out"].T, dtype=f32),
        wwt=np.ascontiguousarray(inputs["L1_attn_w_write"].T, dtype=f32),
        wrt=np.ascontiguousarray(inputs["L1_attn_w_read"].T, dtype=f32),
        dec=np.ascontiguousarray(
            np.asarray(inputs["L1_attn_decay"], dtype=f32).reshape(1, 1)),
        wl1t=np.ascontiguousarray(inputs["L1_ffn_weight"].T, dtype=f32),
        bl1c=np.ascontiguousarray(
            np.asarray(inputs["L1_ffn_bias"], dtype=f32).reshape(8, 128).T),
    )
    wint = np.zeros((1024, 65 * 128), dtype=f32)
    wint[:, :8208] = np.asarray(inputs["L1_attn_w_in"], dtype=f32).T
    shared["wint"] = wint

    in_maps = []
    for core in range(NCORE):
        b, g = core // GRP, core % GRP
        m = dict(shared)
        m["xb"] = x[b]
        m["xo"] = np.ascontiguousarray(x[b, g * SEG:(g + 1) * SEG])
        lsel = np.zeros((128, 4), dtype=f32)
        rsel = np.zeros((128, 4), dtype=f32)
        if g > 0:
            lsel[:, g - 1] = 1.0
        if g < GRP - 1:
            rsel[:, g + 1] = 1.0
        m["lsel"], m["rsel"] = lsel, rsel
        ag = np.zeros((16, 4), dtype=f32)
        ag[:, :g] = 1.0
        m["hgate"] = ag
        m["hgate1"] = np.ascontiguousarray(1.0 - ag)
        agc = np.zeros((128, 4), dtype=f32)
        agc[:, :g] = 1.0
        m["hgatec"] = agc
        we = np.full((1, 4), 1e9, dtype=f32)
        for s in range(g):
            we[0, s] = 2048.0 * (g - 1 - s)
        m["wexp"] = we
        in_maps.append(m)
    return in_maps


def build_nc():
    nc = bass.Bass("TRN2", target_bir_lowering=False)
    EI = dict(kind="ExternalInput")
    dram = {}
    for nm, sh in [("xb", [L, D]), ("xo", [SEG, D]), ("w1t", [D, DHID]),
                   ("w2t", [DHID, D]), ("cw0", [128, 96]), ("cb0", [128, 32]),
                   ("nw", [1, D]), ("wint", [D, 65 * 128]), ("cw1", [128, 64]),
                   ("cb1", [128, 16]), ("alog", [16, 1]), ("dtb", [16, 1]),
                   ("dsk", [128, 16]), ("wot", [DI, D]), ("wwt", [D, D]),
                   ("wrt", [D, D]), ("dec", [1, 1]), ("wl1t", [D, D]),
                   ("bl1c", [128, 8]), ("lsel", [128, 4]), ("rsel", [128, 4]),
                   ("hgate", [16, 4]), ("hgate1", [16, 4]), ("wexp", [1, 4]),
                   ("hgatec", [128, 4])]:
        dram[nm] = nc.dram_tensor(nm, sh, F32, **EI).ap()
    out_hbm = nc.dram_tensor("out", [SEG, D], F32, kind="ExternalOutput").ap()

    kT = nc.dram_tensor("kT", [64, 128, 8, 128], F32).ap()
    vT = nc.dram_tensor("vT", [L, D], F32).ap()
    wq1S = nc.dram_tensor("wq1S", [D, DHID], BF16).ap()
    wq2S = nc.dram_tensor("wq2S", [DHID, D], BF16).ap()
    hS = nc.dram_tensor("hS", [32, 128, SEG], F32).ap()
    pS = nc.dram_tensor("pS", [65, 128, SEG], F32).ap()
    x3S = nc.dram_tensor("x3S", [SEG, D], F32).ap()
    s1S = nc.dram_tensor("s1S", [SEG, D], F32).ap()
    s2S = nc.dram_tensor("s2S", [SEG, D], F32).ap()
    wq4S = nc.dram_tensor("wq4S", [D, D], BF16).ap()
    vtS = nc.dram_tensor("vtS", [NCH, 2, 128, D], F32).ap()

    ccd = nc.dram_tensor("ccd", [512, 2064], F32).ap()
    cc = {}
    for nm, w, grp8 in [("H", 24, False), ("V", 32, False), ("S", 2064, False),
                        ("O", 24, False), ("W", 8192, False), ("M", 24, True)]:
        n_sl = NCORE if grp8 else GRP
        cc[nm + "i"] = nc.dram_tensor("cc" + nm + "i", [128, w], F32).ap()
        cc[nm + "o"] = nc.dram_tensor("cc" + nm + "o", [n_sl * 128, w],
                                      F32).ap()

    dbg = {}
    if DEBUG:
        for nm in ("d_s1", "d_s2", "d_s3"):
            dbg[nm] = nc.dram_tensor(nm, [SEG, D], F32,
                                     kind="ExternalOutput").ap()
        dbg["d_st"] = nc.dram_tensor("d_st", [128, 48], F32,
                                     kind="ExternalOutput").ap()
        dbg["d_ln0"] = nc.dram_tensor("d_ln0", [128, 1024], F32,
                                      kind="ExternalOutput").ap()
        for nm in ("d_v0", "d_k0", "d_p0", "d_ob", "d_nrm"):
            dbg[nm] = nc.dram_tensor(nm, [128, 1024], F32,
                                     kind="ExternalOutput").ap()
        dbg["d_h0"] = nc.dram_tensor("d_h0", [128, 2048], F32,
                                     kind="ExternalOutput").ap()
        dbg["d_row1"] = nc.dram_tensor("d_row1", [1, 2176], F32,
                                       kind="ExternalOutput").ap()
        dbg["d_row2"] = nc.dram_tensor("d_row2", [1, 2176], F32,
                                       kind="ExternalOutput").ap()
        dbg["d_xq"] = nc.dram_tensor("d_xq", [128, 2050], F32,
                                     kind="ExternalOutput").ap()
        dbg["d_hq"] = nc.dram_tensor("d_hq", [128, 258], F32,
                                     kind="ExternalOutput").ap()
        dbg["d_tm"] = nc.dram_tensor("d_tm", [128, 32], F32,
                                     kind="ExternalOutput").ap()
        dbg["d_hall"] = nc.dram_tensor("d_hall", [32 * 128, 2048], F32,
                                       kind="ExternalOutput").ap()
        dbg["d_Y"] = nc.dram_tensor("d_Y", [128, 4096], F32,
                                    kind="ExternalOutput").ap()
        dbg["d_cum"] = nc.dram_tensor("d_cum", [16, 2048], F32,
                                      kind="ExternalOutput").ap()
        dbg["d_aux"] = nc.dram_tensor("d_aux", [128, 4096], F32,
                                      kind="ExternalOutput").ap()
        for nm in ("d_pj0", "d_pj16", "d_pj32", "d_pj48", "d_pj64",
                   "d_opm", "d_inner", "d_xn"):
            dbg[nm] = nc.dram_tensor(nm, [128, 2048], F32,
                                     kind="ExternalOutput").ap()
        dbg["d_mem"] = nc.dram_tensor("d_mem", [128, 4096], F32,
                                      kind="ExternalOutput").ap()
        dbg["d_wf"] = nc.dram_tensor("d_wf", [128, 2048], F32,
                                     kind="ExternalOutput").ap()
        dbg["d_wkp"] = nc.dram_tensor("d_wkp", [128, 2048], F32,
                                      kind="ExternalOutput").ap()
        dbg["d_mmt"] = nc.dram_tensor("d_mmt", [128, 1024], F32,
                                      kind="ExternalOutput").ap()
        dbg["d_oi"] = nc.dram_tensor("d_oi", [128, 24], F32,
                                     kind="ExternalOutput").ap()
        dbg["d_st2"] = nc.dram_tensor("d_st2", [128, 96], F32,
                                      kind="ExternalOutput").ap()
        dbg["d_oo"] = nc.dram_tensor("d_oo", [512, 24], F32,
                                     kind="ExternalOutput").ap()

    sb = lambda nm, sh, dt=F32: nc.alloc_sbuf_tensor(nm, sh, dt).ap()
    ident = sb("ident", [128, 128])
    cmask = sb("cmask", [128, 2, 256])
    cmul = sb("cmul", [128, 2, 256])
    identb = sb("identb", [128, 64]).bitcast(BF16)
    ones1 = sb("ones1", [1, 128])
    onescol = sb("onescol", [128, 1])
    xbuf = sb("xbuf", [128, 2, 1024])
    big1 = sb("big1", [128, 16 * 1024])
    big2 = sb("big2", [128, 16896])
    aux = sb("aux", [128, 8 * 1024])
    stat = sb("stat", [128, 48])
    cst = sb("cst", [128, 48])
    smallc = sb("smallc", [128, 256])
    row1 = sb("row1", [1, 2176])
    row2 = sb("row2", [1, 2176])
    hrow2 = sb("hrow2", [16, 2048])
    hcol = sb("hcol", [16, 32])
    PS = [nc.alloc_psum_tensor(f"PS{i}", [128, 512], F32).ap()
          for i in range(8)]

    with nc.Block():
        TE = E(nc, nc.tensor, "te")
        VE = E(nc, nc.vector, "ve")
        SE = E(nc, nc.scalar, "se")
        GE = E(nc, nc.gpsimd, "ge")
        SY = E(nc, nc.sync, "sy")
        ALL = (TE, VE, SE, GE, SY)

        def barrier():
            toks = []
            for e in ALL:
                if e.last is None:
                    e.do(e.eng.nop())
                toks.append(e.token())
                toks.append(e.dtoken())
            for e in ALL:
                e.wait(*toks)

        def collective(key, grp8=False):
            groups = [list(range(8))] if grp8 else [[0, 1, 2, 3], [4, 5, 6, 7]]
            GE.do(nc.gpsimd.collective_compute(
                "AllGather", OP.bypass, replica_groups=groups,
                ins=[cc[key + "i"].opt()], outs=[cc[key + "o"].opt()]))
            return GE.token()

        def cc_settle():
            for _ in range(2):
                t = SY.dma(ccd, cc["So"])
                SY.wait(t)

        GE.do(nc.gpsimd.memset(ident, 0.0))
        GE.do(nc.gpsimd.affine_select(out=ident, in_=ident,
                                      compare_op=OP.not_equal, fill=1.0,
                                      base=0, pattern=[[-1, 128]],
                                      channel_multiplier=1))
        GE.do(nc.gpsimd.memset(ones1, 1.0))
        GE.do(nc.gpsimd.memset(onescol, 1.0))
        GE.do(nc.gpsimd.memset(cmask, 0.0))
        GE.do(nc.gpsimd.memset(cmul, 0.0))
        for tb in range(2):
            GE.do(nc.gpsimd.affine_select(
                out=cmask[:, tb, :], in_=cmask[:, tb, :],
                compare_op=OP.is_ge, fill=-1e30,
                base=-128 * tb, pattern=[[1, 256]], channel_multiplier=-1))
            GE.do(nc.gpsimd.affine_select(
                out=cmul[:, tb, :], in_=cmul[:, tb, :],
                compare_op=OP.is_ge, fill=1.0,
                base=128 * tb, pattern=[[-1, 256]], channel_multiplier=1))
        GE.do(nc.gpsimd.memset(cst[:, 0:1], 1e-5))
        GE.do(nc.gpsimd.memset(cst[:, 2:3], 1e-6))
        GE.do(nc.gpsimd.memset(cst[:, 1:2], 1e-6))
        t_const = GE.token()
        TE.wait(t_const)
        SE.wait(t_const)
        VE.wait(t_const)
        SY.wait(t_const)
        VE.do(nc.vector.tensor_copy(identb, ident))
        t_identb = VE.token()
        TE.wait(t_identb)

        def emit_ln(src, dst, s0, scr, t_src, t_dst_free=None,
                    t_stats_free=None, eps_col=None, nfeat=1024.0):
            st_ = stat
            VE.wait(t_src, t_stats_free)
            VE.do(nc.vector.tensor_reduce(st_[:, 8:9], src, axis=AX.X,
                                          op=OP.add))
            VE.fence()
            VE.do(nc.vector.tensor_scalar(st_[:, 0:1], st_[:, 8:9],
                                          -1.0 / nfeat, None, OP.mult))
            t_nmu = VE.token()
            SE.wait(t_src, t_nmu)
            SE.do(nc.scalar.activation(scr, src, AF.Square, bias=st_[:, 0:1],
                                       scale=1.0, accum_out=st_[:, 2:3]))
            SE.fence()
            SE.do(nc.scalar.activation(st_[:, 4:5], st_[:, 2:3], AF.Sqrt,
                                       bias=eps_col if eps_col is not None
                                       else cst[:, 0:1], scale=1.0 / nfeat))
            t_sqrt = SE.token()
            VE.wait(t_sqrt, t_dst_free)
            VE.do(nc.vector.reciprocal(st_[:, 6:7], st_[:, 4:5]))
            VE.fence()
            VE.do(nc.vector.tensor_scalar(dst, src, st_[:, 0:1], st_[:, 6:7],
                                          OP.add, OP.mult))
            return VE.token(), t_sqrt

        def emit_logp(lnsrc, lpdst, s0, scr, t_ln):
            st_ = stat
            VE.wait(t_ln)
            VE.do(nc.vector.reduce_max(st_[:, 10:11], lnsrc, axis=AX.X))
            VE.fence()
            VE.do(nc.vector.tensor_scalar(st_[:, 12:13], st_[:, 10:11], -1.0,
                                          None, OP.mult))
            t_mx = VE.token()
            SE.wait(t_mx)
            SE.do(nc.scalar.activation(scr, lnsrc, AF.Exp, bias=st_[:, 12:13],
                                       scale=1.0, accum_out=st_[:, 14:15]))
            SE.fence()
            SE.do(nc.scalar.activation(st_[:, 16:17], st_[:, 14:15], AF.Ln))
            t_lse = SE.token()
            VE.wait(t_lse)
            VE.do(nc.vector.tensor_tensor(st_[:, 18:19], st_[:, 10:11],
                                          st_[:, 16:17], op=OP.add))
            VE.fence()
            VE.do(nc.vector.tensor_scalar(st_[:, 20:21], st_[:, 18:19], -1.0,
                                          None, OP.mult))
            VE.fence()
            VE.do(nc.vector.tensor_scalar(lpdst, lnsrc, st_[:, 20:21], None,
                                          OP.add))
            return VE.token(), t_lse

        xt = aux[:, 0:1024]
        sq = aux[:, 1024:2048]
        lnt = aux[:, 2048:3072]
        lpt = aux[:, 3072:4096]
        tbufs = [big2[:, 0:1024], big2[:, 1024:2048]]
        t_xt_free = None
        t_lnt_free = None
        t_lpt_free = None
        t_stats_free = None
        t_tbuf_free = [None, None]
        t_cp_prev2 = [None, None]
        t_ln_spill = None
        for jt in range(64):
            pp = jt % 2
            tbuf = tbufs[pp]
            SY.wait(t_xt_free)
            t_x = SY.dma(xt, dram["xb"][jt * 128:(jt + 1) * 128, :])
            VE.wait(t_ln_spill)
            t_ln, t_sf = emit_ln(xt, lnt, 0, sq, t_x,
                                 t_dst_free=t_lnt_free,
                                 t_stats_free=t_stats_free)
            t_xt_free = [t_ln, t_sf]
            SY.wait(t_ln)
            t_ln_spill = SY.dma(vT[jt * 128:(jt + 1) * 128, :], lnt)
            VE.wait(t_lpt_free)
            t_logp, t_lse = emit_logp(lnt, lpt, 0, sq, t_ln)
            t_stats_free = t_lse
            t_lnt_free = t_lse
            if DEBUG and jt == 0:
                SY.wait(t_logp, t_lse)
                SY.dma(dbg["d_st"], stat)
                SY.dma(dbg["d_ln0"], lnt)
            TE.wait(t_logp, t_tbuf_free[pp])
            for c in range(8):
                bk = 4 + 2 * pp + (c % 2)
                TE.wait(t_cp_prev2[c % 2])
                TE.do(nc.tensor.transpose(PS[bk][:, 0:128],
                                          lpt[:, c * 128:(c + 1) * 128],
                                          ident))
                t_tr = TE.token()
                SE.wait(t_tr)
                SE.do(nc.scalar.copy(tbuf[:, c * 128:(c + 1) * 128],
                                     PS[bk][:, 0:128]))
                t_cp_prev2[c % 2] = SE.token()
            t_lpt_free = SE.token()
            SY.wait(t_lpt_free)
            t_tbuf_free[pp] = SY.dma(kT[jt],
                                     tbuf.rearrange("p (c j) -> p c j", c=8))
        barrier()

        p_fm = big2.rearrange("p (c t) -> p c t", c=8)
        obuf = big1.rearrange("p (tt d) -> p tt d", tt=16)
        recnrm = stat[:, 24:40]
        nrmacc = cst[:, 16:48]
        VE.do(nc.vector.memset(big1, 0.0))
        VE.do(nc.vector.memset(nrmacc, 0.0))
        t_xt_free = None
        t_lnt_free = None
        t_stats_free = None
        t_p_free = None
        t_cp2 = [None, None]
        for tt in range(16):
            SY.wait(t_xt_free)
            t_x = SY.dma(xt, dram["xo"][tt * 128:(tt + 1) * 128, :])
            t_ln, t_sf = emit_ln(xt, lnt, 0, sq, t_x, t_dst_free=t_lnt_free,
                                 t_stats_free=t_stats_free)
            t_xt_free = [t_ln, t_sf]
            VE.wait(t_lpt_free)
            t_logp, t_lse = emit_logp(lnt, lpt, 0, sq, t_ln)
            t_stats_free = t_lse
            SE.wait(t_logp)
            SE.do(nc.scalar.activation(lnt, lpt, AF.Exp))
            t_p = SE.token()
            t_lpt_free = t_p
            TE.wait(t_p)
            for c in range(8):
                bk = 4 + (c % 2)
                TE.wait(t_cp2[c % 2])
                TE.do(nc.tensor.transpose(PS[bk][:, 0:128],
                                          lnt[:, c * 128:(c + 1) * 128],
                                          ident))
                t_tr = TE.token()
                VE.wait(t_tr)
                VE.do(nc.vector.tensor_copy(
                    p_fm[:, c, tt * 128:(tt + 1) * 128], PS[bk][:, 0:128]))
                t_cp2[c % 2] = VE.token()
            t_lnt_free = TE.token()
        barrier()

        kbufs = [aux[:, 0:1024].rearrange("p (c j) -> p c j", c=8),
                 aux[:, 1024:2048].rearrange("p (c j) -> p c j", c=8),
                 aux[:, 2048:3072].rearrange("p (c j) -> p c j", c=8),
                 aux[:, 3072:4096].rearrange("p (c j) -> p c j", c=8)]
        vbufs = [aux[:, 4096:5120], aux[:, 5120:6144],
                 aux[:, 6144:7168], aux[:, 7168:8192]]
        esb = [xbuf[:, 0, 0:256], xbuf[:, 1, 0:256]]
        t_kv_free = {}
        t_es_free = [None, None]
        t_psS_free = [None, None]
        t_psO_free = None
        for sbi in range(32):
            pp = sbi % 2
            SY.wait(t_kv_free.get(pp))
            tk = [SY.dma(kbufs[2 * pp], kT[2 * sbi]),
                  SY.dma(kbufs[2 * pp + 1], kT[2 * sbi + 1])]
            tv = [SY.dma(vbufs[2 * pp], vT[sbi * 256:sbi * 256 + 128, :]),
                  SY.dma(vbufs[2 * pp + 1],
                         vT[sbi * 256 + 128:sbi * 256 + 256, :])]
            for q in range(8):
                for jj in range(2):
                    kb = kbufs[2 * pp + jj]
                    psS = PS[jj][:, 0:256]
                    TE.wait(tk[jj], t_psS_free[jj])
                    for dch in range(8):
                        TE.do(nc.tensor.matmul(
                            psS, kb[:, dch, :],
                            p_fm[:, dch, q * 256:(q + 1) * 256],
                            start=(dch == 0), stop=(dch == 7)))
                    t_sc = TE.token()
                    SE.wait(t_sc, t_es_free[jj])
                    SE.do(nc.scalar.activation(esb[jj], psS, AF.Exp))
                    t_es = SE.token()
                    t_psS_free[jj] = t_es
                    TE.wait(t_es, tv[jj])
                    if jj == 0:
                        TE.wait(t_psO_free)
                    vb = vbufs[2 * pp + jj]
                    for qs in range(2):
                        lhs = esb[jj][:, qs * 128:(qs + 1) * 128]
                        for dt2 in range(2):
                            TE.do(nc.tensor.matmul(
                                PS[2 + 2 * qs + dt2], lhs,
                                vb[:, dt2 * 512:(dt2 + 1) * 512],
                                start=(jj == 0), stop=(jj == 1)))
                        TE.do(nc.tensor.matmul(
                            PS[6 + qs][:, 0:1], lhs, onescol,
                            start=(jj == 0), stop=(jj == 1)))
                    t_es_free[jj] = TE.token()
                t_vmm = TE.token()
                VE.wait(t_vmm)
                for qs in range(2):
                    tq = 2 * q + qs
                    for dt2 in range(2):
                        VE.do(nc.vector.tensor_tensor(
                            obuf[:, tq, dt2 * 512:(dt2 + 1) * 512],
                            obuf[:, tq, dt2 * 512:(dt2 + 1) * 512],
                            PS[2 + 2 * qs + dt2], op=OP.add))
                    VE.do(nc.vector.tensor_tensor(
                        nrmacc[:, 2 * tq:2 * tq + 1],
                        nrmacc[:, 2 * tq:2 * tq + 1],
                        PS[6 + qs][:, 0:1], op=OP.add))
                t_psO_free = VE.token()
            t_kv_free[pp] = t_psO_free
        VE.fence()
        VE.do(nc.vector.reciprocal(
            recnrm,
            nrmacc.rearrange("p (t two) -> p t two", two=2)[:, :, 0]))
        VE.fence()
        barrier()

        t_s1_free = [None, None]
        for tt in range(16):
            pp = tt % 2
            t_x = SY.dma(aux[:, 0:1024],
                         dram["xo"][tt * 128:(tt + 1) * 128, :])
            VE.wait(t_x, t_s1_free[pp])
            VE.do(nc.vector.scalar_tensor_tensor(
                xbuf[:, pp, :], obuf[:, tt, :], recnrm[:, tt:tt + 1],
                aux[:, 0:1024], op0=OP.mult, op1=OP.add))
            t_s1 = VE.token()
            SY.wait(t_s1)
            t_s1_free[pp] = SY.dma(s1S[tt * 128:(tt + 1) * 128, :],
                                   xbuf[:, pp, :])
        barrier()

        CM = {}
        cm_base = 2064

        def cm_cell(i):
            return row2[0:1, cm_base + 2 * i:cm_base + 2 * i + 1]

        def bcast_cell(cell, dstcol, t_cell):
            TE.wait(t_cell, t_psfree.get(7))
            TE.do(nc.tensor.matmul(PS[7][:, 0:1], ones1, cell,
                                   start=True, stop=True))
            t_mm = TE.token()
            VE.wait(t_mm)
            VE.do(nc.vector.tensor_copy(dstcol, PS[7][:, 0:1]))
            t_psfree[7] = VE.token()
            return t_psfree[7]

        t_psfree = {}
        wbuf0 = aux[:, 0:4096]
        wbufs = [wbuf0, wbuf0]

        def weight_mean(src_d, nslab, slabw, acc_col, scr_col):
            VE.do(nc.vector.memset(stat[:, 44:45], 0.0))
            t_acc = VE.token()
            t_wfree = [None, None]
            for s in range(nslab):
                pp = s % 2
                SY.wait(t_wfree[pp])
                t_w = SY.dma(wbufs[pp][:, 0:slabw],
                             src_d[s * 128:(s + 1) * 128, 0:slabw])
                VE.wait(t_w)
                VE.do(nc.vector.tensor_reduce(stat[:, 42:43],
                                              wbufs[pp][:, 0:slabw],
                                              axis=AX.X, op=OP.add,
                                              apply_absolute_value=True))
                VE.fence()
                VE.do(nc.vector.tensor_tensor(stat[:, 44:45], stat[:, 44:45],
                                              stat[:, 42:43], op=OP.add))
                VE.fence()
                t_wfree[pp] = VE.token()
            t_acc = VE.token()
            TE.wait(t_acc, t_psfree.get(6))
            TE.do(nc.tensor.transpose(PS[6][0:1, 0:128], stat[:, 44:45],
                                      ident))
            t_tr = TE.token()
            VE.wait(t_tr)
            VE.do(nc.vector.tensor_reduce(scr_col, PS[6][0:1, 0:128],
                                          axis=AX.X, op=OP.add))
            VE.fence()
            t_psfree[6] = VE.token()
            return t_psfree[6]

        t_m = weight_mean(dram["w1t"], 8, 4096, None, cm_cell(0))
        VE.wait(t_m)
        VE.do(nc.vector.tensor_scalar(cm_cell(1), cm_cell(0),
                                      1.0 / (1024 * 4096), 1e-5,
                                      OP.mult, OP.max))
        VE.fence()
        VE.do(nc.vector.reciprocal(cm_cell(2), cm_cell(1)))
        VE.fence()
        t_rc = VE.token()
        t_rc1col = bcast_cell(cm_cell(2), cst[:, 40:41], t_rc)
        wqv0 = aux.bitcast(BF16)[:, 8192:12288]
        wqi0 = aux.bitcast(I32)[:, 6144:8192]
        t_q_free = None
        for s in range(8):
            pp = s % 2
            SY.wait(t_q_free)
            t_w = SY.dma(wbufs[pp], dram["w1t"][s * 128:(s + 1) * 128, :])
            GE.wait(t_w, t_rc1col, t_q_free)
            GE.do(nc.gpsimd.tensor_scalar(wqi0[:, 0:2048], wbufs[pp][:, 0:2048],
                                          cst[:, 40:41], None, OP.mult))
            GE.do(nc.gpsimd.tensor_scalar(wqv0[:, 0:2048], wqi0[:, 0:2048],
                                          -1.0, 1.0, OP.max, OP.min))
            GE.do(nc.gpsimd.tensor_scalar(wqi0[:, 0:2048],
                                          wbufs[pp][:, 2048:4096],
                                          cst[:, 40:41], None, OP.mult))
            GE.do(nc.gpsimd.tensor_scalar(wqv0[:, 2048:4096], wqi0[:, 0:2048],
                                          -1.0, 1.0, OP.max, OP.min))
            t_q = GE.token()
            SY.wait(t_q)
            t_q_free = SY.dma(wq1S[s * 128:(s + 1) * 128, :], wqv0)
        SY.wait(t_q_free)
        GE.wait(t_q_free)
        t_m = weight_mean(dram["w2t"], 32, 1024, None, cm_cell(4))
        VE.wait(t_m)
        VE.do(nc.vector.tensor_scalar(cm_cell(5), cm_cell(4),
                                      1.0 / (1024 * 4096), 1e-5,
                                      OP.mult, OP.max))
        VE.fence()
        VE.do(nc.vector.reciprocal(cm_cell(6), cm_cell(5)))
        VE.fence()
        t_rc = VE.token()
        t_rc2col = bcast_cell(cm_cell(6), cst[:, 41:42], t_rc)
        t_q_free = None
        for s in range(32):
            pp = s % 2
            SY.wait(t_q_free)
            t_w = SY.dma(wbufs[pp][:, 0:1024],
                         dram["w2t"][s * 128:(s + 1) * 128, :])
            GE.wait(t_w, t_rc2col, t_q_free)
            GE.do(nc.gpsimd.tensor_scalar(wqi0[:, 0:1024],
                                          wbufs[pp][:, 0:1024],
                                          cst[:, 41:42], None, OP.mult))
            GE.do(nc.gpsimd.tensor_scalar(wqv0[:, 0:1024], wqi0[:, 0:1024],
                                          -1.0, 1.0, OP.max, OP.min))
            t_q = GE.token()
            SY.wait(t_q)
            t_q_free = SY.dma(wq2S[s * 128:(s + 1) * 128, :], wqv0[:, 0:1024])
        SY.wait(t_q_free)
        GE.wait(t_q_free)
        t_m = weight_mean(dram["wl1t"], 8, 1024, None, cm_cell(8))
        VE.wait(t_m)
        VE.do(nc.vector.tensor_scalar(cm_cell(9), cm_cell(8),
                                      1.0 / (1024 * 1024), None, OP.mult))
        VE.fence()
        VE.do(nc.vector.tensor_scalar(cm_cell(10), cm_cell(9), 0.5, None,
                                      OP.mult))
        VE.fence()
        t_thr = VE.token()
        t_thrcol = bcast_cell(cm_cell(10), cst[:, 42:43], t_thr)
        t_q_free = None
        for s in range(8):
            pp = s % 2
            SY.wait(t_q_free)
            t_w = SY.dma(wbufs[pp][:, 0:1024],
                         dram["wl1t"][s * 128:(s + 1) * 128, :])
            SE.wait(t_w, t_q_free)
            SE.do(nc.scalar.activation(wbufs[pp][:, 1024:2048],
                                       wbufs[pp][:, 0:1024], AF.Abs))
            t_abs = SE.token()
            VE.wait(t_abs, t_thrcol, t_q_free)
            VE.do(nc.vector.tensor_scalar(wbufs[pp][:, 2048:3072],
                                          wbufs[pp][:, 1024:2048],
                                          cst[:, 42:43], None, OP.is_gt))
            t_msk = VE.token()
            SE.wait(t_msk)
            SE.do(nc.scalar.activation(wbufs[pp][:, 1024:2048],
                                       wbufs[pp][:, 0:1024], AF.Sign))
            t_sgn = SE.token()
            VE.wait(t_sgn)
            VE.do(nc.vector.tensor_tensor(wqv0[:, 0:1024],
                                          wbufs[pp][:, 2048:3072],
                                          wbufs[pp][:, 1024:2048],
                                          op=OP.mult))
            t_q = VE.token()
            SY.wait(t_q)
            t_q_free = SY.dma(wq4S[s * 128:(s + 1) * 128, :], wqv0[:, 0:1024])
        SY.wait(t_q_free)
        GE.wait(t_q_free)
        t_sc = [SY.dma(smallc[:, 0:96], dram["cw0"]),
                SY.dma(smallc[:, 96:128], dram["cb0"]),
                SY.dma(smallc[:, 128:192], dram["cw1"]),
                SY.dma(smallc[:, 192:208], dram["cb1"]),
                SY.dma(smallc[:, 208:224], dram["dsk"]),
                SY.dma(smallc[:, 224:232], dram["bl1c"]),
                SY.dma(smallc[:, 244:248], dram["lsel"]),
                SY.dma(smallc[:, 248:252], dram["rsel"]),
                SY.dma(smallc[:, 240:244], dram["hgatec"]),
                SY.dma(hcol[0:16, 0:1], dram["alog"]),
                SY.dma(hcol[0:16, 2:3], dram["dtb"]),
                SY.dma(hcol[0:16, 4:8], dram["hgate"]),
                SY.dma(hcol[0:16, 8:12], dram["hgate1"]),
                SY.dma(row2[0:1, 2056:2060], dram["wexp"]),
                SY.dma(cm_cell(12), dram["dec"])]
        barrier()

        xqfm = big2.bitcast(BF16)[:, 0:16400].rearrange(
            "p (c t) -> p c t", c=8)
        xt = aux[:, 0:1024]
        sq = aux[:, 1024:2048]
        lnt = aux[:, 2048:3072]
        qi = aux.bitcast(I32)[:, 3072:4096]
        qb = aux.bitcast(BF16)[:, 8192:9216]
        t_xt_free = None
        t_stats_free = None
        t_lnt_free = None
        t_qb_free = None
        t_qprev = None
        t_cp2 = [None, None]
        for tt in range(16):
            SY.wait(t_xt_free)
            t_x = SY.dma(xt, s1S[tt * 128:(tt + 1) * 128, :])
            t_ln, t_sf = emit_ln(xt, lnt, 0, sq, t_x, t_dst_free=t_lnt_free,
                                 t_stats_free=t_stats_free)
            t_xt_free = [t_ln, t_sf]
            t_stats_free = t_sf
            VE.wait(t_ln, t_qprev)
            VE.do(nc.vector.reduce_max(stat[:, 24:25], lnt, axis=AX.X,
                                       apply_absolute_value=True))
            VE.fence()
            VE.do(nc.vector.tensor_scalar(stat[:, 26:27], stat[:, 24:25],
                                          1e-5, None, OP.max))
            VE.fence()
            VE.do(nc.vector.reciprocal(stat[:, 28:29], stat[:, 26:27]))
            VE.fence()
            VE.do(nc.vector.tensor_scalar(stat[:, 30:31], stat[:, 28:29],
                                          127.0, None, OP.mult))
            VE.fence()
            t_s = VE.token()
            GE.wait(t_ln, t_s, t_qb_free)
            GE.do(nc.gpsimd.tensor_scalar(qi, lnt, stat[:, 30:31], None,
                                          OP.mult))
            GE.do(nc.gpsimd.tensor_scalar(qb, qi, -128.0, 127.0,
                                          OP.max, OP.min))
            t_q = GE.token()
            t_lnt_free = t_q
            t_qprev = t_q
            TE.wait(t_q)
            for c in range(8):
                bk = 2 + (c % 2)
                TE.wait(t_cp2[c % 2])
                TE.do(nc.tensor.transpose(PS[bk].bitcast(BF16)[:, 0:128],
                                          qb[:, c * 128:(c + 1) * 128],
                                          identb))
                t_tr = TE.token()
                VE.wait(t_tr)
                VE.do(nc.vector.tensor_copy(
                    xqfm[:, c, 1 + tt * 128:1 + (tt + 1) * 128],
                    PS[bk].bitcast(BF16)[:, 0:128]))
                t_cp2[c % 2] = VE.token()
            t_qb_free = TE.token()
            TE.wait(t_psfree.get(4))
            TE.do(nc.tensor.transpose(PS[4][0:1, 0:128], stat[:, 26:27],
                                      ident))
            t_tr = TE.token()
            VE.wait(t_tr)
            VE.do(nc.vector.tensor_copy(row1[0:1, 1 + tt * 128:1 + (tt + 1) * 128],
                                        PS[4][0:1, 0:128]))
            t_psfree[4] = VE.token()
        barrier()

        hpk = aux[:, 0:24]
        VE.do(nc.vector.tensor_copy(
            hpk[:, 0:8], xqfm[:, :, 1:2].rearrange("p c one -> p (c one)")))
        VE.do(nc.vector.tensor_copy(
            hpk[:, 8:16], xqfm[:, :, 2048:2049].rearrange("p c one -> p (c one)")))
        VE.do(nc.vector.tensor_copy(hpk[0:1, 16:17], row1[0:1, 1:2]))
        VE.do(nc.vector.tensor_copy(hpk[0:1, 17:18], row1[0:1, 2048:2049]))
        t_pk = VE.token()
        SY.wait(t_pk)
        t_pkd = SY.dma(cc["Hi"], hpk)
        GE.wait(t_pkd)
        t_cc = collective("H")
        SY.wait(t_cc)
        cc_settle()
        t_gl2 = SY.dma(aux[:, 160:256].rearrange("p (s c) -> p s c", s=4),
                       cc["Ho"].rearrange("(s p) c -> p s c", p=128))
        gath = aux[:, 160:256].rearrange("p (s c) -> p s c", s=4)
        VE.wait(t_gl2)
        VE.do(nc.vector.tensor_scalar(hpk[:, 0:8], gath[:, 0, 8:16],
                                      smallc[:, 244:245], None, OP.mult))
        for s in range(1, 4):
            VE.fence()
            VE.do(nc.vector.scalar_tensor_tensor(
                hpk[:, 0:8], gath[:, s, 8:16], smallc[:, 244 + s:245 + s],
                hpk[:, 0:8], op0=OP.mult, op1=OP.add))
        VE.fence()
        VE.do(nc.vector.tensor_scalar(hpk[:, 8:16], gath[:, 0, 0:8],
                                      smallc[:, 248:249], None, OP.mult))
        for s in range(1, 4):
            VE.fence()
            VE.do(nc.vector.scalar_tensor_tensor(
                hpk[:, 8:16], gath[:, s, 0:8], smallc[:, 248 + s:249 + s],
                hpk[:, 8:16], op0=OP.mult, op1=OP.add))
        VE.fence()
        VE.do(nc.vector.tensor_copy(
            xqfm[:, :, 0:1].rearrange("p c one -> p (c one)"), hpk[:, 0:8]))
        VE.do(nc.vector.tensor_copy(
            xqfm[:, :, 2049:2050].rearrange("p c one -> p (c one)"),
            hpk[:, 8:16]))
        VE.do(nc.vector.tensor_scalar(row1[0:1, 0:1], gath[0:1, 0, 17:18],
                                      smallc[0:1, 244:245], None, OP.mult))
        for s in range(1, 4):
            VE.fence()
            VE.do(nc.vector.scalar_tensor_tensor(
                row1[0:1, 0:1], gath[0:1, s, 17:18], smallc[0:1, 244 + s:245 + s],
                row1[0:1, 0:1], op0=OP.mult, op1=OP.add))
        VE.fence()
        VE.do(nc.vector.tensor_scalar(row1[0:1, 2049:2050],
                                      gath[0:1, 0, 16:17],
                                      smallc[0:1, 248:249], None, OP.mult))
        for s in range(1, 4):
            VE.fence()
            VE.do(nc.vector.scalar_tensor_tensor(
                row1[0:1, 2049:2050], gath[0:1, s, 16:17],
                smallc[0:1, 248 + s:249 + s], row1[0:1, 2049:2050],
                op0=OP.mult, op1=OP.add))
        VE.fence()
        VE.do(nc.vector.tensor_scalar(row2[0:1, 0:2050], row1[0:1, 0:2050],
                                      cm_cell(1), 1.0 / 127.0,
                                      OP.mult, OP.mult))
        VE.fence()
        t_drow = VE.token()
        barrier()
        if DEBUG:
            VE.do(nc.vector.tensor_copy(aux[:, 5636:7686], xqfm[:, 0, :]))
            t_c = VE.token()
            SY.wait(t_c)
            SY.wait(SY.dma(dbg["d_xq"], aux[:, 5636:7686]))
            SY.wait(SY.dma(dbg["d_row1"], row1))
            SY.wait(SY.dma(dbg["d_row2"], row2))
            barrier()

        wsl = [aux.bitcast(BF16)[:, 0:1024].rearrange("p (c j) -> p c j", c=8),
               aux.bitcast(BF16)[:, 1024:2048].rearrange("p (c j) -> p c j",
                                                         c=8)]
        DBC = aux[:, 1024:3074]
        hdq = aux[:, 3074:3332]
        cts = [aux[:, 3332:3588], aux[:, 5636:5892]]
        shX = aux[:, 3588:5636]
        tokmax = cst[:, 16:48]
        VE.do(nc.vector.memset(tokmax, 0.0))
        for wti in range(5):
            w0 = wti * 512
            w1 = min(2050, w0 + 512)
            TE.wait(t_drow, t_psfree.get(5))
            TE.do(nc.tensor.matmul(PS[5][:, 0:w1 - w0], ones1,
                                   row2[0:1, w0:w1], start=True, stop=True))
            t_mm = TE.token()
            VE.wait(t_mm)
            VE.do(nc.vector.tensor_copy(DBC[:, w0:w1], PS[5][:, 0:w1 - w0]))
            t_psfree[5] = VE.token()
        t_dbc = VE.token()
        t_wsl_free = [None, None]
        t_hdq_free = None
        t_shx_free = None
        t_psb_free = [None, None]
        t_tm_free = [None, None]
        t_silu_free = [None, None]
        for c in range(32):
            pp = c % 2
            SY.wait(t_wsl_free[pp])
            for kk in range(8):
                t_w = SY.dma(wsl[pp][:, kk, :],
                             wq1S[kk * 128:(kk + 1) * 128,
                                  c * 128:(c + 1) * 128])
            for tm in range(8):
                bq = tm % 2
                w0 = tm * 256
                TE.wait(t_w, t_psb_free[bq])
                for dch in range(8):
                    TE.do(nc.tensor.matmul(PS[bq][:, 0:258],
                                           wsl[pp][:, dch, :],
                                           xqfm[:, dch, w0:w0 + 258],
                                           start=(dch == 0), stop=(dch == 7)))
                t_mm = TE.token()
                VE.wait(t_mm, t_dbc, t_hdq_free)
                VE.do(nc.vector.tensor_tensor(hdq, PS[bq][:, 0:258],
                                              DBC[:, w0:w0 + 258], op=OP.mult))
                t_psb_free[bq] = VE.token()
                ct1 = cts[bq]
                VE.wait(t_silu_free[bq])
                VE.do(nc.vector.tensor_scalar(ct1, hdq[:, 0:256],
                                              smallc[:, 3 * c:3 * c + 1],
                                              None, OP.mult))
                VE.fence()
                VE.do(nc.vector.scalar_tensor_tensor(
                    ct1, hdq[:, 1:257], smallc[:, 3 * c + 1:3 * c + 2], ct1,
                    op0=OP.mult, op1=OP.add))
                VE.fence()
                VE.do(nc.vector.scalar_tensor_tensor(
                    ct1, hdq[:, 2:258], smallc[:, 3 * c + 2:3 * c + 3], ct1,
                    op0=OP.mult, op1=OP.add))
                t_cv = VE.token()
                t_hdq_free = t_cv
                SE.wait(t_cv, t_shx_free)
                SE.do(nc.scalar.activation(shX[:, w0:w0 + 256], ct1, AF.Silu,
                                           bias=smallc[:, 96 + c:97 + c],
                                           scale=1.0))
                t_silu_free[bq] = SE.token()
            t_silu = SE.token()
            SY.wait(t_silu)
            t_spl = SY.dma(hS[c], shX)
            for ttp in range(16):
                bk = 2 + (ttp % 2)
                TE.wait(t_silu, t_tm_free[ttp % 2])
                TE.do(nc.tensor.transpose(PS[bk][:, 0:128],
                                          shX[:, ttp * 128:(ttp + 1) * 128],
                                          ident))
                t_tr = TE.token()
                VE.wait(t_tr)
                VE.do(nc.vector.tensor_reduce(stat[:, 42:43], PS[bk][:, 0:128],
                                              axis=AX.X, op=OP.add
                                              if False else OP.max,
                                              apply_absolute_value=True))
                VE.fence()
                VE.do(nc.vector.tensor_tensor(
                    tokmax[:, 2 * ttp:2 * ttp + 1],
                    tokmax[:, 2 * ttp:2 * ttp + 1],
                    stat[:, 42:43], op=OP.max))
                VE.fence()
                t_tm_free[ttp % 2] = VE.token()
            t_shx_free = [TE.token(), t_spl]
            t_wsl_free[pp] = TE.token()
        barrier()

        for ttp in range(16):
            VE.do(nc.vector.tensor_scalar(stat[:, 42:43],
                                          tokmax[:, 2 * ttp:2 * ttp + 1],
                                          1e-5, None, OP.max))
            VE.fence()
            t_cl = VE.token()
            TE.wait(t_cl, t_psfree.get(4))
            TE.do(nc.tensor.transpose(PS[4][0:1, 0:128], stat[:, 42:43],
                                      ident))
            t_tr = TE.token()
            VE.wait(t_tr)
            VE.do(nc.vector.tensor_copy(
                row1[0:1, ttp * 128:(ttp + 1) * 128], PS[4][0:1, 0:128]))
            t_psfree[4] = VE.token()
        VE.fence()
        VE.do(nc.vector.reciprocal(row2[0:1, 0:2048], row1[0:1, 0:2048]))
        VE.fence()
        VE.do(nc.vector.tensor_scalar(row2[0:1, 0:2048], row2[0:1, 0:2048],
                                      127.0, None, OP.mult))
        VE.fence()
        VE.do(nc.vector.tensor_scalar(row1[0:1, 0:2048], row1[0:1, 0:2048],
                                      cm_cell(5), 1.0 / 127.0,
                                      OP.mult, OP.mult))
        VE.fence()
        t_rows = VE.token()
        wsl2 = [aux.bitcast(BF16)[:, 0:4096].rearrange("p (c j) -> p c j",
                                                       c=32),
                aux.bitcast(BF16)[:, 4096:8192].rearrange("p (c j) -> p c j",
                                                          c=32)]
        S2BC = aux[:, 4096:4608]
        D2BC = aux[:, 4608:5120]
        qtmp = aux[:, 5120:5632]
        q2b = big1.bitcast(BF16)[:, 0:16384].rearrange("p (c t) -> p c t",
                                                       c=32)
        o2f = big1[:, 8192:12288].rearrange("p (c t) -> p c t", c=8)
        t_hsl_free = None
        t_q2_free = None
        t_q2b_free = None
        t_o2_free = None
        t_w2sl_free = [None, None]
        hsl = big2[:, 0:16384].rearrange("p (c t) -> p c t", c=32)
        t_xb_free = [None, None]
        for sl in range(4):
            t0 = sl * 512
            TE.wait(t_rows, t_psfree.get(5))
            TE.do(nc.tensor.matmul(PS[5], ones1, row2[0:1, t0:t0 + 512],
                                   start=True, stop=True))
            t_mm = TE.token()
            VE.wait(t_mm)
            VE.do(nc.vector.tensor_copy(S2BC, PS[5]))
            t_psfree[5] = VE.token()
            TE.wait(t_psfree[5])
            TE.do(nc.tensor.matmul(PS[5], ones1, row1[0:1, t0:t0 + 512],
                                   start=True, stop=True))
            t_mm = TE.token()
            VE.wait(t_mm)
            VE.do(nc.vector.tensor_copy(D2BC, PS[5]))
            t_psfree[5] = VE.token()
            t_bc2 = VE.token()
            SY.wait(t_hsl_free)
            GE.wait(t_q2b_free)
            for kk in range(4):
                t_h = SY.dma(hsl[:, 8 * kk:8 * kk + 8, :],
                             hS[8 * kk:8 * kk + 8, :, t0:t0 + 512].rearrange(
                                 "c p t -> p c t"))
            for c in range(32):
                VE.wait(t_h, t_bc2, t_q2_free)
                VE.do(nc.vector.tensor_tensor(qtmp, hsl[:, c, :], S2BC,
                                              op=OP.mult))
                t_qt = VE.token()
                GE.wait(t_qt)
                GE.do(nc.gpsimd.tensor_scalar(qi[:, 0:512]
                                              if False else
                                              aux.bitcast(I32)[:, 5632:6144],
                                              qtmp, 0.0, None, OP.add))
                GE.do(nc.gpsimd.tensor_scalar(
                    q2b[:, c, :], aux.bitcast(I32)[:, 5632:6144],
                    -128.0, 127.0, OP.max, OP.min))
                t_q2_free = GE.token()
            t_hsl_free = GE.token()
            for o in range(8):
                pp = o % 2
                SY.wait(t_w2sl_free[pp])
                for kk in range(8):
                    t_w2 = SY.dma(
                        wsl2[pp][:, 4 * kk:4 * kk + 4, :],
                        wq2S[kk * 512:(kk + 1) * 512,
                             o * 128:(o + 1) * 128].rearrange(
                            "(c p) j -> p c j", p=128))
                bq = o % 2
                TE.wait(t_w2, t_hsl_free, t_psb_free[bq])
                for c in range(32):
                    TE.do(nc.tensor.matmul(PS[bq], wsl2[pp][:, c, :],
                                           q2b[:, c, :],
                                           start=(c == 0), stop=(c == 31)))
                t_mm2 = TE.token()
                t_w2sl_free[pp] = t_mm2
                VE.wait(t_mm2, t_o2_free)
                VE.do(nc.vector.tensor_tensor(o2f[:, o, :], PS[bq], D2BC,
                                              op=OP.mult))
                t_psb_free[bq] = VE.token()
            t_q2b_free = TE.token()
            t_deq2 = VE.token()
            for ttp in range(4):
                tt = sl * 4 + ttp
                ppx = ttp % 2
                SY.wait(t_xb_free[ppx])
                t_x = SY.dma(xbuf[:, ppx, :], s1S[tt * 128:(tt + 1) * 128, :])
                for och in range(8):
                    bk = 2 + (och % 2)
                    TE.wait(t_deq2, t_tm_free[och % 2])
                    TE.do(nc.tensor.transpose(
                        PS[bk][:, 0:128],
                        o2f[:, och, ttp * 128:(ttp + 1) * 128], ident))
                    t_tr = TE.token()
                    VE.wait(t_tr, t_x)
                    VE.do(nc.vector.tensor_tensor(
                        xbuf[:, ppx, och * 128:(och + 1) * 128],
                        xbuf[:, ppx, och * 128:(och + 1) * 128],
                        PS[bk][:, 0:128], op=OP.add))
                    t_tm_free[och % 2] = VE.token()
                t_s2 = VE.token()
                SY.wait(t_s2)
                t_xb_free[ppx] = SY.dma(s2S[tt * 128:(tt + 1) * 128, :],
                                        xbuf[:, ppx, :])
            t_o2_free = TE.token()
        barrier()

        xnfm = big2[:, 0:16408].rearrange("p (c t) -> p c t", c=8)
        wsl3 = [aux[:, 0:1024].rearrange("p (c j) -> p c j", c=8),
                aux[:, 1024:2048].rearrange("p (c j) -> p c j", c=8)]
        NWB = aux[:, 2048:3072]
        xt = aux[:, 3072:4096]
        sq = aux[:, 4096:5120]
        ln3 = aux[:, 5120:6144]
        xnt = aux[:, 6144:7168]
        vcv = aux[:, 7168:7424]
        vct = aux[:, 7424:7680]
        t_nw = SY.dma(row1[0:1, 0:1024], dram["nw"])
        TE.wait(t_nw, t_psfree.get(0))
        TE.do(nc.tensor.matmul(PS[0], ones1, row1[0:1, 0:512],
                               start=True, stop=True))
        t_mm = TE.token()
        VE.wait(t_mm)
        VE.do(nc.vector.tensor_copy(NWB[:, 0:512], PS[0]))
        t_psfree[0] = VE.token()
        TE.wait(t_psfree[0])
        TE.do(nc.tensor.matmul(PS[0], ones1, row1[0:1, 512:1024],
                               start=True, stop=True))
        t_mm = TE.token()
        VE.wait(t_mm)
        VE.do(nc.vector.tensor_copy(NWB[:, 512:1024], PS[0]))
        t_psfree[0] = VE.token()
        t_xt_free = None
        t_stats_free = None
        t_ln3_free = None
        t_xn_free = None
        t_cp2 = [None, None]
        for tt in range(16):
            SY.wait(t_xt_free)
            t_x = SY.dma(xt, s2S[tt * 128:(tt + 1) * 128, :])
            t_ln, t_sf = emit_ln(xt, ln3, 0, sq, t_x, t_dst_free=t_ln3_free,
                                 t_stats_free=t_stats_free)
            t_xt_free = [t_ln, t_sf]
            t_stats_free = t_sf
            SE.wait(t_ln)
            SE.do(nc.scalar.activation(sq, ln3, AF.Square, bias=0.0,
                                       scale=1.0, accum_out=stat[:, 24:25]))
            SE.fence()
            SE.do(nc.scalar.activation(stat[:, 26:27], stat[:, 24:25],
                                       AF.Sqrt, bias=cst[:, 2:3],
                                       scale=1.0 / 1024.0))
            t_rms = SE.token()
            VE.wait(t_rms, t_xn_free)
            VE.do(nc.vector.reciprocal(stat[:, 28:29], stat[:, 26:27]))
            VE.fence()
            VE.do(nc.vector.tensor_scalar(xnt, ln3, stat[:, 28:29], None,
                                          OP.mult))
            VE.do(nc.vector.tensor_tensor(xnt, xnt, NWB, op=OP.mult))
            t_xn = VE.token()
            t_ln3_free = t_xn
            TE.wait(t_xn)
            for c in range(8):
                bk = c % 2
                TE.wait(t_cp2[bk])
                TE.do(nc.tensor.transpose(PS[bk][:, 0:128],
                                          xnt[:, c * 128:(c + 1) * 128],
                                          ident))
                t_tr = TE.token()
                VE.wait(t_tr)
                VE.do(nc.vector.tensor_copy(
                    xnfm[:, c, 3 + tt * 128:3 + (tt + 1) * 128],
                    PS[bk][:, 0:128]))
                t_cp2[bk] = VE.token()
            t_xn_free = TE.token()
        barrier()
        hpk = aux[:, 0:32]
        VE.do(nc.vector.tensor_copy(
            hpk[:, 0:24].rearrange("p (c t) -> p c t", c=8),
            xnfm[:, :, 2048:2051]))
        t_pk = VE.token()
        SY.wait(t_pk)
        t_pkd = SY.dma(cc["Vi"], hpk[:, 0:32])
        GE.wait(t_pkd)
        t_cc = collective("V")
        SY.wait(t_cc)
        cc_settle()
        t_gl = SY.dma(aux[:, 128:256].rearrange("p (s c) -> p s c", s=4),
                      cc["Vo"].rearrange("(s p) c -> p s c", p=128))
        gath = aux[:, 128:256].rearrange("p (s c) -> p s c", s=4)
        VE.wait(t_gl)
        VE.do(nc.vector.tensor_scalar(hpk[:, 0:24], gath[:, 0, 0:24],
                                      smallc[:, 244:245], None, OP.mult))
        for s in range(1, 4):
            VE.fence()
            VE.do(nc.vector.scalar_tensor_tensor(
                hpk[:, 0:24], gath[:, s, 0:24], smallc[:, 244 + s:245 + s],
                hpk[:, 0:24], op0=OP.mult, op1=OP.add))
        VE.fence()
        VE.do(nc.vector.tensor_copy(
            xnfm[:, :, 0:3],
            hpk[:, 0:24].rearrange("p (c t) -> p c t", c=8)))
        VE.fence()
        barrier()
        if DEBUG:
            VE.do(nc.vector.tensor_copy(aux[:, 1024:3072],
                                        xnfm[:, 0, 0:2048]))
            t_c = VE.token()
            SY.wait(t_c)
            t_w = SY.dma(dbg["d_xn"], aux[:, 1024:3072])
            VE.wait(t_w)
            barrier()

        t_wsl_free = [None, None]
        t_psb_free = [None, None]
        t_ev_free = None
        t_ct_free = [None, None]
        for oc in range(65):
            pp = oc % 2
            ow = 128 if oc < 64 else 128
            SY.wait(t_wsl_free[pp])
            for kk in range(8):
                t_w = SY.dma(wsl3[pp][:, kk, :],
                             dram["wint"][kk * 128:(kk + 1) * 128,
                                          oc * 128:(oc + 1) * 128])
            for q in range(8):
                bq = 5 + (q % 2)
                if oc < 16:
                    w0, wd = 256 * q, 259
                else:
                    w0, wd = 3 + 256 * q, 256
                TE.wait(t_w, t_psb_free[q % 2])
                for dch in range(8):
                    TE.do(nc.tensor.matmul(PS[bq][:, 0:wd],
                                           wsl3[pp][:, dch, :],
                                           xnfm[:, dch, w0:w0 + wd],
                                           start=(dch == 0), stop=(dch == 7)))
                t_mm = TE.token()
                if oc < 16:
                    ct = vcv if q % 2 == 0 else vct
                    VE.wait(t_mm, t_ct_free[q % 2])
                    VE.do(nc.vector.tensor_copy(sq[:, 0:260][:, 0:259],
                                                PS[bq][:, 0:259]))
                    VE.fence()
                    VE.do(nc.vector.tensor_scalar(
                        ct, sq[:, 0:256], smallc[:, 128 + 4 * oc:129 + 4 * oc],
                        None, OP.mult))
                    for kk2 in range(1, 4):
                        VE.fence()
                        VE.do(nc.vector.scalar_tensor_tensor(
                            ct, sq[:, kk2:256 + kk2],
                            smallc[:, 128 + 4 * oc + kk2:129 + 4 * oc + kk2],
                            ct, op0=OP.mult, op1=OP.add))
                    t_cv = VE.token()
                    t_psb_free[q % 2] = t_cv
                    SE.wait(t_cv, t_ev_free)
                    SE.do(nc.scalar.activation(
                        xt[:, 0:256], ct, AF.Silu,
                        bias=smallc[:, 192 + oc:193 + oc], scale=1.0))
                    t_sl = SE.token()
                    t_ct_free[q % 2] = t_sl
                    SY.wait(t_sl)
                    t_ev_free = SY.dma(pS[oc][:, 256 * q:256 * (q + 1)],
                                       xt[:, 0:256])
                elif oc < 32:
                    SE.wait(t_mm, t_ev_free)
                    SE.do(nc.scalar.activation(xt[:, 0:256], PS[bq][:, 0:256],
                                               AF.Silu))
                    t_sl = SE.token()
                    t_psb_free[q % 2] = t_sl
                    SY.wait(t_sl)
                    t_ev_free = SY.dma(pS[oc][:, 256 * q:256 * (q + 1)],
                                       xt[:, 0:256])
                elif oc < 64:
                    SE.wait(t_mm, t_ev_free)
                    SE.do(nc.scalar.copy(xt[:, 0:256], PS[bq][:, 0:256]))
                    t_sl = SE.token()
                    t_psb_free[q % 2] = t_sl
                    SY.wait(t_sl)
                    t_ev_free = SY.dma(pS[oc][:, 256 * q:256 * (q + 1)],
                                       xt[:, 0:256])
                else:
                    SE.wait(t_mm, t_ev_free)
                    SE.do(nc.scalar.copy(xt[0:16, 0:256], PS[bq][0:16, 0:256]))
                    t_sl = SE.token()
                    t_psb_free[q % 2] = t_sl
                    SY.wait(t_sl)
                    t_ev_free = SY.dma(pS[64][0:16, 256 * q:256 * (q + 1)],
                                       xt[0:16, 0:256])
            t_wsl_free[pp] = TE.token()
        barrier()

        dtb_c = hcol[0:16, 2:3]
        hrow = aux[0:16, 0:2048]
        dsc1 = aux[0:16, 2048:4096]
        dsc2 = aux[0:16, 4096:6144]
        t_dt = SY.dma(hrow, pS[64][0:16, :])
        VE.wait(t_dt)
        VE.do(nc.vector.tensor_scalar(hrow, hrow, dtb_c, None, OP.add))
        VE.fence()
        VE.do(nc.vector.tensor_scalar(hrow2, hrow, 0.0, None, OP.max))
        VE.fence()
        VE.do(nc.vector.tensor_tensor(dsc1, hrow, hrow2, op=OP.subtract))
        VE.fence()
        VE.do(nc.vector.tensor_scalar(dsc2, hrow2, -1.0, None, OP.mult))
        t_sp = VE.token()
        SE.wait(t_sp)
        SE.do(nc.scalar.activation(dsc1, dsc1, AF.Exp))
        SE.fence()
        SE.do(nc.scalar.activation(dsc2, dsc2, AF.Exp))
        t_ex = SE.token()
        VE.wait(t_ex)
        VE.do(nc.vector.tensor_tensor(dsc1, dsc1, dsc2, op=OP.add))
        t_ad = VE.token()
        SE.wait(t_ad)
        SE.do(nc.scalar.activation(dsc1, dsc1, AF.Ln))
        t_ln2 = SE.token()
        VE.wait(t_ln2)
        VE.do(nc.vector.tensor_tensor(hrow, hrow2, dsc1, op=OP.add))
        VE.fence()
        SE.do(nc.scalar.activation(hcol[0:16, 12:13], hcol[0:16, 0:1], AF.Exp))
        t_al = SE.token()
        VE.wait(t_al)
        VE.do(nc.vector.tensor_scalar(hcol[0:16, 14:15], hcol[0:16, 12:13],
                                      -1.0, None, OP.mult))
        VE.fence()
        VE.do(nc.vector.tensor_scalar(hrow, hrow, hcol[0:16, 14:15], None,
                                      OP.mult))
        VE.fence()
        GE.do(nc.gpsimd.memset(aux[0:16, 6144:6400], 1.0))
        GE.do(nc.gpsimd.memset(hcol[0:16, 16:17], 0.0))
        t_on = GE.token()
        VE.wait(t_on)
        for c in range(8):
            VE.do(nc.vector.tensor_tensor_scan(
                hrow2[:, 256 * c:256 * (c + 1)], aux[0:16, 6144:6400],
                hrow[:, 256 * c:256 * (c + 1)], hcol[0:16, 16:17],
                op0=OP.mult, op1=OP.add))
        VE.fence()
        VE.do(nc.vector.tensor_reduce(hcol[0:16, 18:19], hrow, axis=AX.X,
                                      op=OP.add))
        VE.fence()
        t_ps2 = VE.token()
        SE.wait(t_ps2)
        SE.do(nc.scalar.activation(hcol[0:16, 20:21], hcol[0:16, 18:19],
                                   AF.Exp))
        t_pseg = SE.token()
        barrier()

        one11 = ones1[0:1, 0:1]

        def row2col(rowap, colap, t_row, bk, n=128):
            TE.wait(t_row, t_psfree.get(bk))
            TE.do(nc.tensor.matmul(PS[bk][0:n, 0:1], rowap, one11,
                                   start=True, stop=True))
            t = TE.token()
            VE.wait(t)
            VE.do(nc.vector.tensor_copy(colap, PS[bk][0:n, 0:1]))
            t_psfree[bk] = VE.token()
            return t_psfree[bk]

        def bcast_row(rowap, dst, t_row, bk, n):
            TE.wait(t_row, t_psfree.get(bk))
            TE.do(nc.tensor.matmul(PS[bk][:, 0:n], ones1, rowap,
                                   start=True, stop=True))
            t = TE.token()
            VE.wait(t)
            VE.do(nc.vector.tensor_copy(dst, PS[bk][:, 0:n]))
            t_psfree[bk] = VE.token()
            return t_psfree[bk]

        hstf = xbuf.rearrange("p a b -> p (a b)")

        def hsth(h):
            return hstf[:, h * 128:(h + 1) * 128]

        VE.do(nc.vector.memset(hstf, 0.0))
        vbuf = big2[:, 0:4096].rearrange("p (h t) -> p h t", h=16)
        bbuf = big2[:, 4096:8192].rearrange("p (h t) -> p h t", h=16)
        cbuf = big2[:, 8192:12288].rearrange("p (h t) -> p h t", h=16)
        gbuf = big2[:, 12288:16384].rearrange("p (h t) -> p h t", h=16)
        CEb = aux[:, 4096:4352]
        CBb = aux[:, 4352:4608]
        ccol = aux[:, 4608:4612]
        wcolb = aux[:, 4612:4616]
        dcol = aux[:, 4616:4617]
        hsc = aux[:, 4618:4746]
        Bts = aux[:, 4746:5002].rearrange("p (b j) -> p b j", b=2)
        vTt = aux[:, 5002:5258].rearrange("p (b j) -> p b j", b=2)
        Ebuf = aux[:, 5258:5770].rearrange("p (b j) -> p b j", b=2)
        MTb = aux[:, 5770:6282].rearrange("p (b j) -> p b j", b=2)
        csb = aux[:, 6282:6538]
        ddg = aux[:, 6538:6666]
        ddb = aux[:, 6666:6922]
        Ybuf = aux[:, 0:4096].rearrange("p (h t) -> p h t", h=16)
        rowe = row1[0:1, 0:256]
        rowE = row1[0:1, 256:512]
        cell = row1[0:1, 512:513]
        crow = row1[0:1, 576:832]

        t_ld_free = None
        t_unit = None
        for c in range(NCH):
            SY.wait(t_ld_free)
            for h in range(16):
                SY.dma(vbuf[:, h, :], pS[h][:, 256 * c:256 * (c + 1)])
            for h in range(16):
                SY.dma(bbuf[:, h, :], pS[32 + h][:, 256 * c:256 * (c + 1)])
            t_ld = SY.dtoken()
            TE.wait(t_ld)
            VE.wait(t_ld)
            for h in range(NH):
                t_j = [TE.token(), VE.token()]
                TE.wait(t_j[1], t_psfree.get(1))
                TE.do(nc.tensor.matmul(PS[1][0:1, 0:256],
                                       ident[0:16, h:h + 1],
                                       hrow2[0:16, 256 * c:256 * (c + 1)],
                                       start=True, stop=True))
                t_cx = TE.token()
                VE.wait(t_cx)
                VE.do(nc.vector.tensor_copy(crow, PS[1][0:1, 0:256]))
                t_psfree[1] = VE.token()
                t_crw = VE.token()
                SE.wait(t_j[0], t_crw, t_pseg)
                SE.do(nc.scalar.activation(rowe, crow, AF.Exp, scale=-1.0,
                                           bias=crow[0:1, 255:256]))
                SE.fence()
                SE.do(nc.scalar.activation(cell, crow[0:1, 255:256], AF.Exp))
                t_rows = SE.token()
                t1 = row2col(rowe[0:1, 0:128], wcolb[:, 0:1], t_rows, 0)
                t2 = row2col(rowe[0:1, 128:256], wcolb[:, 2:3], t_rows, 1)
                t3 = bcast_row(cell, dcol, t_rows, 0, 1)
                t_vev = None
                for tb in range(2):
                    TE.wait(t_j[1], t_vev)
                    TE.do(nc.tensor.transpose(
                        PS[2 + tb][:, 0:128],
                        bbuf[:, h, tb * 128:(tb + 1) * 128], ident))
                    t_tr = TE.token()
                    VE.wait(t_tr, t1, t2)
                    VE.do(nc.vector.tensor_scalar(
                        Bts[:, tb, :], PS[2 + tb][:, 0:128],
                        wcolb[:, 2 * tb:2 * tb + 1], None, OP.mult))
                    t_e1 = VE.token()
                    TE.wait(t_e1)
                    TE.do(nc.tensor.transpose(
                        PS[2 + tb][:, 0:128],
                        vbuf[:, h, tb * 128:(tb + 1) * 128], ident))
                    t_tr = TE.token()
                    VE.wait(t_tr)
                    VE.do(nc.vector.tensor_copy(vTt[:, tb, :],
                                                PS[2 + tb][:, 0:128]))
                    t_vev = VE.token()
                t_tv = VE.token()
                VE.wait(t3)
                VE.do(nc.vector.tensor_scalar(hsc, hsth(h), dcol, None,
                                              OP.mult))
                t_hs = VE.token()
                TE.wait(t_tv, t_hs, t_unit)
                TE.do(nc.tensor.matmul(PS[4][:, 0:128], Bts[:, 0, :],
                                       vTt[:, 0, :], start=True, stop=False))
                TE.do(nc.tensor.matmul(PS[4][:, 0:128], Bts[:, 1, :],
                                       vTt[:, 1, :], start=False, stop=False))
                TE.do(nc.tensor.matmul(PS[4][:, 0:128], ident, hsc,
                                       start=False, stop=True))
                t_mm = TE.token()
                VE.wait(t_mm)
                VE.do(nc.vector.tensor_copy(hsth(h), PS[4][:, 0:128]))
                t_unit = VE.token()
            t_ld_free = t_unit
        barrier()

        VE.do(nc.vector.tensor_copy(aux[:, 0:2048], hstf))
        VE.do(nc.vector.memset(aux[:, 2048:2064], 0.0))
        VE.do(nc.vector.tensor_copy(aux[0:16, 2048:2049], hcol[0:16, 20:21]))
        t_pk = VE.token()
        SY.wait(t_pk)
        t_pkd = SY.dma(cc["Si"], aux[:, 0:2064])
        GE.wait(t_pkd)
        t_cc = collective("S")
        SY.wait(t_cc)
        cc_settle()
        gathS = big2[:, 0:8256].rearrange("p (s c) -> p s c", s=4)
        t_gl = SY.dma(gathS, cc["So"].rearrange("(s p) c -> p s c", p=128))
        VE.wait(t_gl)
        VE.do(nc.vector.memset(hstf, 0.0))
        VE.fence()
        for s in range(4):
            VE.do(nc.vector.scalar_tensor_tensor(
                hcol[0:16, 22:23], gathS[0:16, s, 2048:2049],
                hcol[0:16, 4 + s:5 + s], hcol[0:16, 8 + s:9 + s],
                op0=OP.mult, op1=OP.add))
            VE.fence()
            t_cf = VE.token()
            TE.wait(t_cf, t_psfree.get(0))
            TE.do(nc.tensor.transpose(PS[0][0:1, 0:16], hcol[0:16, 22:23],
                                      ident[0:16, 0:16]))
            t_tr = TE.token()
            VE.wait(t_tr)
            VE.do(nc.vector.tensor_copy(row1[0:1, 520:536], PS[0][0:1, 0:16]))
            t_psfree[0] = VE.token()
            t_rw = VE.token()
            tcb = bcast_row(row1[0:1, 520:536], aux[:, 2064:2080], t_rw, 1, 16)
            VE.wait(tcb)
            for h in range(16):
                VE.do(nc.vector.tensor_scalar(
                    hsc, gathS[:, s, h * 128:(h + 1) * 128],
                    smallc[:, 240 + s:241 + s], None, OP.mult))
                VE.fence()
                VE.do(nc.vector.scalar_tensor_tensor(
                    hsth(h), hsth(h), aux[:, 2064 + h:2065 + h], hsc,
                    op0=OP.mult, op1=OP.add))
                VE.fence()
        barrier()

        out_fm = big1.rearrange("p (c t) -> p c t", c=8)
        t_ld_free = None
        t_unit = None
        t_wosl_free = None
        for c in range(NCH):
            SY.wait(t_ld_free)
            for h in range(16):
                SY.dma(vbuf[:, h, :], pS[h][:, 256 * c:256 * (c + 1)])
            for h in range(16):
                SY.dma(gbuf[:, h, :], pS[16 + h][:, 256 * c:256 * (c + 1)])
            for h in range(16):
                SY.dma(bbuf[:, h, :], pS[32 + h][:, 256 * c:256 * (c + 1)])
            for h in range(16):
                SY.dma(cbuf[:, h, :], pS[48 + h][:, 256 * c:256 * (c + 1)])
            t_ld = SY.dtoken()
            TE.wait(t_ld)
            VE.wait(t_ld)
            SE.wait(t_ld)
            GE.wait(t_ld)
            for h in range(NH):
                t_j = [TE.token(), VE.token(), GE.token()]
                TE.wait(t_j[1], t_psfree.get(1))
                TE.do(nc.tensor.matmul(PS[1][0:1, 0:256],
                                       ident[0:16, h:h + 1],
                                       hrow2[0:16, 256 * c:256 * (c + 1)],
                                       start=True, stop=True))
                t_cx = TE.token()
                VE.wait(t_cx)
                VE.do(nc.vector.tensor_copy(crow, PS[1][0:1, 0:256]))
                t_psfree[1] = VE.token()
                t_crw = VE.token()
                SE.wait(t_j[0], t_j[2], t_crw)
                SE.do(nc.scalar.activation(rowe, crow, AF.Exp, scale=-1.0,
                                           bias=crow[0:1, 255:256]))
                SE.fence()
                SE.do(nc.scalar.activation(rowE, crow, AF.Exp))
                t_rows = SE.token()
                t1 = row2col(rowe[0:1, 0:128], wcolb[:, 0:1], t_rows, 0)
                t2 = row2col(rowe[0:1, 128:256], wcolb[:, 2:3], t_rows, 1)
                tc1 = row2col(crow[0:1, 0:128], ccol[:, 0:1], t_rows, 0)
                tc2 = row2col(crow[0:1, 128:256], ccol[:, 2:3], t_rows, 1)
                tE = bcast_row(rowE, CEb, t_rows, 0, 256)
                tB = bcast_row(crow, CBb, t_rows, 1, 256)
                for tb in range(2):
                    VE.wait(tE, tB, tc1, tc2)
                    VE.do(nc.vector.tensor_scalar(
                        ddb, CBb, ccol[:, 2 * tb:2 * tb + 1], None, OP.subtract))
                    VE.fence()
                    VE.do(nc.vector.tensor_tensor(ddb, ddb, cmask[:, tb, :],
                                                  op=OP.add))
                    t_dd = VE.token()
                    SE.wait(t_dd)
                    SE.do(nc.scalar.activation(Ebuf[:, tb, :], ddb, AF.Exp))
                    t_eb = SE.token()
                    psS = PS[2 + 3 * tb]
                    TE.wait(t_psfree.get(20 + tb))
                    TE.do(nc.tensor.matmul(psS[:, 0:256],
                                           bbuf[:, h, tb * 128:(tb + 1) * 128],
                                           cbuf[:, h, :],
                                           start=True, stop=True))
                    t_sm = TE.token()
                    VE.wait(t_sm, t_eb)
                    VE.do(nc.vector.tensor_tensor(MTb[:, tb, :], psS[:, 0:256],
                                                  Ebuf[:, tb, :], op=OP.mult))
                    t_psfree[20 + tb] = VE.token()
                VE.wait(tE)
                VE.do(nc.vector.tensor_tensor(csb, cbuf[:, h, :], CEb,
                                              op=OP.mult))
                VE.do(nc.vector.tensor_scalar(ddg, ident,
                                              smallc[:, 208 + h:209 + h],
                                              None, OP.mult))
                t_cs = VE.token()
                t_vev = None
                for tb in range(2):
                    TE.wait(t_j[1], t_vev)
                    TE.do(nc.tensor.transpose(
                        PS[3][:, 0:128], bbuf[:, h, tb * 128:(tb + 1) * 128],
                        ident))
                    t_tr = TE.token()
                    VE.wait(t_tr, t1, t2)
                    VE.do(nc.vector.tensor_scalar(
                        Bts[:, tb, :], PS[3][:, 0:128],
                        wcolb[:, 2 * tb:2 * tb + 1], None, OP.mult))
                    t_e1 = VE.token()
                    TE.wait(t_e1)
                    TE.do(nc.tensor.transpose(
                        PS[3][:, 0:128], vbuf[:, h, tb * 128:(tb + 1) * 128],
                        ident))
                    t_tr = TE.token()
                    VE.wait(t_tr)
                    VE.do(nc.vector.tensor_copy(vTt[:, tb, :],
                                                PS[3][:, 0:128]))
                    t_vev = VE.token()
                t_tv = VE.token()
                TE.wait(t_tv, t_cs, t_unit)
                TE.do(nc.tensor.matmul(PS[3][:, 0:256], vTt[:, 0, :],
                                       MTb[:, 0, :], start=True, stop=False))
                TE.do(nc.tensor.matmul(PS[3][:, 0:256], vTt[:, 1, :],
                                       MTb[:, 1, :], start=False, stop=False))
                TE.do(nc.tensor.matmul(PS[3][:, 0:256], hsth(h), csb,
                                       start=False, stop=False))
                TE.do(nc.tensor.matmul(PS[3][:, 0:256], ddg, vbuf[:, h, :],
                                       start=False, stop=True))
                t_ymm = TE.token()
                VE.wait(t_ymm)
                VE.do(nc.vector.tensor_copy(Ybuf[:, h, :], PS[3][:, 0:256]))
                t_yev = VE.token()
                if DEBUG and c == 0 and h == 0:
                    SY.wait(t_yev)
                    t_da = SY.dma(dbg["d_aux"], aux[:, 4096:8192])
                    VE.wait(t_da)
                    TE.wait(t_da)
                VE.do(nc.vector.tensor_scalar(hsc, hsth(h),
                                              CEb[:, 255:256], None, OP.mult))
                t_hs = VE.token()
                TE.wait(t_hs, t_yev)
                TE.do(nc.tensor.matmul(PS[4][:, 0:128], Bts[:, 0, :],
                                       vTt[:, 0, :], start=True, stop=False))
                TE.do(nc.tensor.matmul(PS[4][:, 0:128], Bts[:, 1, :],
                                       vTt[:, 1, :], start=False, stop=False))
                TE.do(nc.tensor.matmul(PS[4][:, 0:128], ident, hsc,
                                       start=False, stop=True))
                t_mm = TE.token()
                VE.wait(t_mm)
                VE.do(nc.vector.tensor_copy(hsth(h), PS[4][:, 0:128]))
                t_unit = VE.token()
            if DEBUG and c == 0:
                VE.wait(t_unit)
                t_dy = SY.dma(dbg["d_Y"], Ybuf.rearrange("p h t -> p (h t)"))
                VE.wait(t_dy)
                SY.wait(t_dy)
            VE.wait(t_unit)
            VE.do(nc.vector.tensor_tensor(gbuf.rearrange("p h t -> p (h t)"),
                                          Ybuf.rearrange("p h t -> p (h t)"),
                                          gbuf.rearrange("p h t -> p (h t)"),
                                          op=OP.mult))
            t_yg = VE.token()
            for oc2 in range(8):
                SY.wait(t_wosl_free, t_yg)
                for kk in range(4):
                    SY.dma(
                        Ybuf[:, 4 * kk:4 * kk + 4, 0:128],
                        dram["wot"].rearrange("(h p) o -> p h o", p=128)
                        [:, 4 * kk:4 * kk + 4, oc2 * 128:(oc2 + 1) * 128])
                t_wo = SY.dtoken()
                TE.wait(t_wo, t_yg, t_psfree.get(26))
                for dic in range(16):
                    TE.do(nc.tensor.matmul(
                        PS[6][:, 0:256],
                        Ybuf[:, dic, 0:128], gbuf[:, dic, :],
                        start=(dic == 0), stop=(dic == 15)))
                t_om = TE.token()
                t_wosl_free = t_om
                VE.wait(t_om)
                VE.do(nc.vector.tensor_copy(out_fm[:, oc2, 256 * c:256 * (c + 1)],
                                            PS[6][:, 0:256]))
                t_psfree[26] = VE.token()
            t_ld_free = VE.token()
        barrier()

        if DEBUG:
            VE.do(nc.vector.tensor_copy(aux[:, 0:2048], out_fm[:, 0, :]))
            t_c = VE.token()
            SY.wait(t_c)
            t_w = SY.dma(dbg["d_opm"], aux[:, 0:2048])
            VE.wait(t_w)
            barrier()

        VE.do(nc.vector.tensor_copy(
            aux[:, 0:8],
            out_fm[:, :, 2047:2048].rearrange("p c one -> p (c one)")))
        VE.do(nc.vector.memset(aux[:, 8:24], 0.0))
        t_pk = VE.token()
        SY.wait(t_pk)
        t_pkd = SY.dma(cc["Oi"], aux[:, 0:24])
        GE.wait(t_pkd)
        t_cc = collective("O")
        SY.wait(t_cc)
        cc_settle()
        if DEBUG:
            t_d1 = SY.dma(dbg["d_oi"], cc["Oi"])
            t_d2 = SY.dma(dbg["d_oo"], cc["Oo"])
            SY.wait(t_d1, t_d2)
        t_gl = SY.dma(aux[:, 832:928].rearrange("p (s c) -> p s c", s=4),
                      cc["Oo"].rearrange("(s p) c -> p s c", p=128))
        gathO = aux[:, 832:928].rearrange("p (s c) -> p s c", s=4)
        wkb = smallc[:, 0:16].rearrange("p (e two) -> p e two", two=2)[:, :, 0]
        VE.wait(t_gl)
        VE.do(nc.vector.tensor_scalar(wkb, gathO[:, 0, 0:8],
                                      smallc[:, 244:245], None, OP.mult))
        for s in range(1, 4):
            VE.fence()
            VE.do(nc.vector.scalar_tensor_tensor(
                wkb, gathO[:, s, 0:8],
                smallc[:, 244 + s:245 + s],
                wkb, op0=OP.mult, op1=OP.add))
        VE.fence()
        SE.do(nc.scalar.activation(cm_cell(14), cm_cell(12), AF.Sigmoid))
        SE.fence()
        SE.do(nc.scalar.activation(cm_cell(16), cm_cell(14), AF.Ln))
        t_lg = SE.token()
        GE.do(nc.gpsimd.memset(row2[0:1, 0:2048], 1.0))
        t_on = GE.token()
        VE.wait(t_on)
        VE.do(nc.vector.tensor_tensor_scan(
            row1[0:1, 0:2048], row2[0:1, 0:2048], row2[0:1, 0:2048],
            -1.0, op0=OP.mult, op1=OP.add))
        VE.fence()
        VE.do(nc.vector.tensor_scalar(row2[0:1, 0:2048], row1[0:1, 0:2048],
                                      -2047.0, -1.0, OP.add, OP.mult))
        VE.fence()
        t_ir = VE.token()
        SE.wait(t_ir)
        SE.do(nc.scalar.activation(row2[0:1, 0:2048], row2[0:1, 0:2048],
                                   AF.Exp, scale=cm_cell(16)))
        t_grow = SE.token()
        growcol2 = stat[:, 0:32]
        for k in range(16):
            row2col(row2[0:1, k * 128:(k + 1) * 128],
                    growcol2[:, 2 * k:2 * k + 1], t_grow, k % 2)
        VE.do(nc.vector.reciprocal(
            cst[:, 16:32],
            growcol2.rearrange("p (k two) -> p k two", two=2)[:, :, 0]))
        t_rg = VE.token()
        rgcol = cst[:, 16:32]
        SE.wait(t_ir)
        SE.do(nc.scalar.activation(row1[0:1, 1024:1280], row1[0:1, 0:256],
                                   AF.Exp, scale=cm_cell(16)))
        t_gp = SE.token()
        SE.do(nc.scalar.activation(cm_cell(18), cm_cell(16), AF.Exp,
                                   scale=256.0))
        t_g256 = SE.token()
        t_g256c = bcast_row(cm_cell(18), cst[:, 43:44], t_g256, 0, 1)
        t_lgc = bcast_row(cm_cell(16), cst[:, 42:43], t_lg, 1, 1)
        MmemT = big2[:, 16384:16896].rearrange("p (b j) -> p b j", b=2)
        tib = bcast_row(row1[0:1, 0:256], aux[:, 64:320], t_ir, 0, 256)
        for tb in range(2):
            tjc = row2col(row1[0:1, tb * 128:(tb + 1) * 128],
                          aux[:, 40:41], t_ir, 1)
            VE.fence()
            VE.wait(tib, tjc)
            VE.do(nc.vector.tensor_scalar(aux[:, 320:576], aux[:, 64:320],
                                          aux[:, 40:41], None, OP.subtract))
            VE.fence()
            VE.do(nc.vector.tensor_scalar(aux[:, 320:576], aux[:, 320:576],
                                          -1.0, None, OP.add))
            VE.fence()
            VE.do(nc.vector.tensor_scalar(aux[:, 320:576], aux[:, 320:576],
                                          0.0, None, OP.max))
            VE.fence()
            VE.do(nc.vector.tensor_scalar(aux[:, 320:576], aux[:, 320:576],
                                          cst[:, 42:43], None, OP.mult))
            t_dm = VE.token()
            SE.wait(t_dm)
            SE.do(nc.scalar.activation(aux[:, 576:832], aux[:, 320:576],
                                       AF.Exp))
            t_em = SE.token()
            VE.wait(t_em)
            VE.do(nc.vector.tensor_tensor(MmemT[:, tb, :], aux[:, 576:832],
                                          cmul[:, tb, :], op=OP.mult))
            VE.fence()
        if DEBUG:
            t_mm2 = VE.token()
            SY.wait(t_mm2)
            t_dmt = SY.dma(dbg["d_mmt"][:, 0:512],
                           MmemT.rearrange("p b j -> p (b j)"))
            t_dmt2 = SY.dma(dbg["d_mmt"][:, 512:1024], aux[:, 0:512])
            SY.wait(t_dmt, t_dmt2)
        barrier()

        wwt_s = big2[:, 0:8192].rearrange("p (e d) -> p e d", e=8)
        dWacc = big2[:, 8192:16384].rearrange("p (e d) -> p e d", e=8)
        for kk in range(8):
            SY.dma(wwt_s[:, kk, :], dram["wwt"][kk * 128:(kk + 1) * 128, :])
        t_wwt = SY.dtoken()
        VE.do(nc.vector.memset(big2[:, 8192:16384], 0.0))
        t_dw0 = VE.token()
        v_a = aux[:, 0:2048].rearrange("p (d t) -> p d t", d=8)
        vt_a = aux[:, 2048:4096].rearrange("p (b d) -> p b d", b=2)
        wkTs = aux[:, 4096:4352].rearrange("p (b j) -> p b j", b=2)
        wkp = xbuf.rearrange("p a b -> p (a b)").rearrange(
            "p (c t) -> p c t", c=8)
        barrier()
        VE.wait(t_gl)
        for ech in range(8):
            VE.do(nc.vector.tensor_copy(wkp[:, ech, 1:256],
                                        out_fm[:, ech, 0:255]))
            VE.do(nc.vector.tensor_copy(wkp[:, ech, 0:1],
                                        smallc[:, 2 * ech:2 * ech + 1]))
        t_wkp = VE.token()
        t_vfree = None
        t_sp_free = None
        for c in range(NCH):
            for dch in range(8):
                bq = 5 + (dch % 2)
                TE.wait(t_wwt, t_psfree.get(30 + dch % 2), t_dw0)
                for ech in range(8):
                    TE.do(nc.tensor.matmul(
                        PS[bq][:, 0:256], wwt_s[:, ech, dch * 128:(dch + 1) * 128],
                        out_fm[:, ech, 256 * c:256 * (c + 1)],
                        start=(ech == 0), stop=(ech == 7)))
                t_vm = TE.token()
                VE.wait(t_vm, t_vfree)
                VE.do(nc.vector.tensor_copy(v_a[:, dch, :], PS[bq][:, 0:256]))
                t_psfree[30 + dch % 2] = VE.token()
            t_vev = VE.token()
            for tb in range(2):
                for dch in range(8):
                    bk = tb
                    TE.wait(t_vev, t_psfree.get(bk))
                    TE.do(nc.tensor.transpose(
                        PS[bk][:, 0:128],
                        v_a[:, dch, tb * 128:(tb + 1) * 128], ident))
                    t_tr = TE.token()
                    VE.wait(t_tr, t_sp_free)
                    VE.do(nc.vector.tensor_copy(
                        vt_a[:, tb, dch * 128:(dch + 1) * 128],
                        PS[bk][:, 0:128]))
                    t_psfree[bk] = VE.token()
            t_vt = VE.token()
            SY.wait(t_vt)
            t_sp_free = SY.dma(vtS[c], vt_a)
            t_vfree = t_vt
            for ech in range(8):
                wk_src = (wkp[:, ech, :] if c == 0 else
                          out_fm[:, ech, 256 * c - 1:256 * (c + 1) - 1])
                for tb in range(2):
                    TE.wait(t_wkp, t_psfree.get(tb))
                    TE.do(nc.tensor.transpose(
                        PS[tb][:, 0:128], wk_src[:, tb * 128:(tb + 1) * 128],
                        ident))
                    t_tr = TE.token()
                    VE.wait(t_tr, t_rg)
                    VE.do(nc.vector.tensor_scalar(
                        wkTs[:, tb, :], PS[tb][:, 0:128],
                        growcol2[:, 2 * (2 * c + tb):2 * (2 * c + tb) + 1],
                        None, OP.mult))
                    t_psfree[tb] = VE.token()
                t_wk = VE.token()
                for dt2 in range(2):
                    bq = 5 + dt2
                    TE.wait(t_wk, t_psfree.get(30 + dt2))
                    TE.do(nc.tensor.matmul(PS[bq][:, 0:512], wkTs[:, 0, :],
                                           vt_a[:, 0, dt2 * 512:(dt2 + 1) * 512],
                                           start=True, stop=False))
                    TE.do(nc.tensor.matmul(PS[bq][:, 0:512], wkTs[:, 1, :],
                                           vt_a[:, 1, dt2 * 512:(dt2 + 1) * 512],
                                           start=False, stop=True))
                    t_wm = TE.token()
                    VE.wait(t_wm)
                    VE.do(nc.vector.tensor_tensor(
                        dWacc[:, ech, dt2 * 512:(dt2 + 1) * 512],
                        dWacc[:, ech, dt2 * 512:(dt2 + 1) * 512],
                        PS[bq][:, 0:512], op=OP.add))
                    t_psfree[30 + dt2] = VE.token()
            t_vfree = VE.token()
        barrier()

        t_pkd = SY.dma(cc["Wi"], big2[:, 8192:16384])
        GE.wait(t_pkd)
        t_cc = collective("W")
        SE.wait(t_lg)
        SE.do(nc.scalar.activation(row1[0:1, 536:540], row2[0:1, 2056:2060],
                                   AF.Exp, scale=cm_cell(16)))
        t_wc = SE.token()
        t_wcb = bcast_row(row1[0:1, 536:540], cst[:, 44:48], t_wc, 0, 4)
        W_s = big2[:, 0:8192].rearrange("p (e d) -> p e d", e=8)
        SY.wait(t_cc)
        cc_settle()
        t_fold = None
        for s in range(4):
            SY.wait(t_fold)
            t_gw = SY.dma(big2[:, 8192:16384],
                          cc["Wo"][s * 128:(s + 1) * 128, :])
            VE.wait(t_gw, t_wcb)
            if s == 0:
                VE.do(nc.vector.tensor_scalar(
                    big2[:, 0:8192], big2[:, 8192:16384], cst[:, 44:45],
                    None, OP.mult))
            else:
                VE.do(nc.vector.scalar_tensor_tensor(
                    big2[:, 0:8192], big2[:, 8192:16384],
                    cst[:, 44 + s:45 + s], big2[:, 0:8192],
                    op0=OP.mult, op1=OP.add))
            VE.fence()
            t_fold = VE.token()
        t_wf = VE.token()
        if DEBUG:
            SY.wait(t_wf)
            t_dwf = SY.dma(dbg["d_wf"], big2[:, 0:2048])
            SY.wait(t_dwf)
            VE.wait(t_dwf)
        wrt_s = big2[:, 8192:16384].rearrange("p (d o) -> p d o", d=8)
        SY.wait(t_wf)
        for kk in range(8):
            SY.dma(wrt_s[:, kk, :], dram["wrt"][kk * 128:(kk + 1) * 128, :])
        t_wrt = SY.dtoken()
        t_gpb = bcast_row(row1[0:1, 1024:1280], aux[:, 2048:2304], t_gp, 0,
                          256)
        barrier()

        rkg = aux[:, 0:2048].rearrange("p (e t) -> p e t", e=8)
        GPB = aux[:, 2048:2304]
        bcsave = aux[:, 2304:2432].rearrange("p (e c) -> p e c", e=8)
        vt_b = aux[:, 2816:4864].rearrange("p (b d) -> p b d", b=2)
        wkTs2 = aux[:, 4864:5120].rearrange("p (b j) -> p b j", b=2)
        SMb = aux[:, 5120:5632].rearrange("p (b j) -> p b j", b=2)
        rbuf = aux[:, 5632:7680].rearrange("p (d t) -> p d t", d=8)
        for c2 in range(7):
            VE.do(nc.vector.tensor_copy(
                bcsave[:, :, 2 * c2:2 * c2 + 1],
                out_fm[:, :, 256 * (c2 + 1) - 1:256 * (c2 + 1)]))
        t_bcs = VE.token()
        t_vtl = None
        t_rk_free = None
        t_rb_free = None
        t_vtb_free = None
        for c in range(NCH):
            SY.wait(t_vtb_free)
            t_vtl = SY.dma(vt_b, vtS[c])
            if c > 0:
                VE.wait(t_bcs, t_rk_free)
                for ech in range(8):
                    VE.do(nc.vector.tensor_copy(
                        wkp[:, ech, 1:256],
                        out_fm[:, ech, 256 * c:256 * (c + 1) - 1]))
                    VE.do(nc.vector.tensor_copy(
                        wkp[:, ech, 0:1], bcsave[:, ech, 2 * c - 2:2 * c - 1]))
            t_wkp = VE.token()
            for ech in range(8):
                VE.wait(t_gpb, t_rk_free)
                VE.do(nc.vector.tensor_tensor(
                    rkg[:, ech, :], out_fm[:, ech, 256 * c:256 * (c + 1)],
                    GPB, op=OP.mult))
            t_rkg = VE.token()
            for tb in range(2):
                psS = PS[2 + 3 * tb]
                TE.wait(t_psfree.get(20 + tb), t_wkp)
                for ech in range(8):
                    TE.do(nc.tensor.matmul(
                        PS[2 + 3 * tb][:, 0:256],
                        wkp[:, ech, tb * 128:(tb + 1) * 128],
                        out_fm[:, ech, 256 * c:256 * (c + 1)],
                        start=(ech == 0), stop=(ech == 7)))
                t_sm = TE.token()
                VE.wait(t_sm)
                VE.do(nc.vector.tensor_tensor(
                    SMb[:, tb, :], psS[:, 0:256], MmemT[:, tb, :],
                    op=OP.mult))
                t_psfree[20 + tb] = VE.token()
            t_smt = VE.token()
            for dch in range(8):
                TE.wait(t_wf, t_rkg, t_smt, t_vtl, t_psfree.get(24))
                for ech in range(8):
                    TE.do(nc.tensor.matmul(
                        PS[3][:, 0:256], W_s[:, ech, dch * 128:(dch + 1) * 128],
                        rkg[:, ech, :], start=(ech == 0), stop=False))
                for tb in range(2):
                    TE.do(nc.tensor.matmul(
                        PS[3][:, 0:256], vt_b[:, tb, dch * 128:(dch + 1) * 128],
                        SMb[:, tb, :], start=False, stop=(tb == 1)))
                t_rm = TE.token()
                VE.wait(t_rm, t_rb_free)
                VE.do(nc.vector.tensor_copy(rbuf[:, dch, :], PS[3][:, 0:256]))
                t_psfree[24] = VE.token()
            t_rev = VE.token()
            if DEBUG and c == 0:
                SY.wait(t_rev)
                t_dm = SY.dma(dbg["d_mem"], aux[:, 4096:8192])
                t_dk = SY.dma(dbg["d_wkp"],
                              wkp.rearrange("p a b -> p (a b)"))
                SY.wait(t_dm, t_dk)
                VE.wait(t_dm, t_dk)
                TE.wait(t_dm, t_dk)
            for ech in range(8):
                for tb in range(2):
                    TE.wait(t_psfree.get(tb), t_rev)
                    TE.do(nc.tensor.transpose(
                        PS[tb][:, 0:128],
                        wkp[:, ech, tb * 128:(tb + 1) * 128], ident))
                    t_tr = TE.token()
                    VE.wait(t_tr, t_rg)
                    VE.do(nc.vector.tensor_scalar(
                        wkTs2[:, tb, :], PS[tb][:, 0:128],
                        growcol2[:, 28 + 2 * tb:29 + 2 * tb],
                        None, OP.mult))
                    t_psfree[tb] = VE.token()
                t_wk = VE.token()
                for dt2 in range(2):
                    bq = 5 + dt2 if False else 6
                    TE.wait(t_wk, t_psfree.get(26))
                    TE.do(nc.tensor.matmul(PS[6][:, 0:512], wkTs2[:, 0, :],
                                           vt_b[:, 0, dt2 * 512:(dt2 + 1) * 512],
                                           start=True, stop=False))
                    TE.do(nc.tensor.matmul(PS[6][:, 0:512], wkTs2[:, 1, :],
                                           vt_b[:, 1, dt2 * 512:(dt2 + 1) * 512],
                                           start=False, stop=True))
                    t_wm = TE.token()
                    VE.wait(t_wm, t_g256c)
                    VE.do(nc.vector.scalar_tensor_tensor(
                        W_s[:, ech, dt2 * 512:(dt2 + 1) * 512],
                        W_s[:, ech, dt2 * 512:(dt2 + 1) * 512],
                        cst[:, 43:44], PS[6][:, 0:512],
                        op0=OP.mult, op1=OP.add))
                    t_psfree[26] = VE.token()
            t_wupd = VE.token()
            t_vtb_free = TE.token()
            for oc3 in range(8):
                TE.wait(t_wrt, t_rev, t_psfree.get(28))
                for dch in range(8):
                    TE.do(nc.tensor.matmul(
                        PS[7][:, 0:256], wrt_s[:, dch, oc3 * 128:(oc3 + 1) * 128],
                        rbuf[:, dch, :], start=(dch == 0), stop=(dch == 7)))
                t_rt = TE.token()
                VE.wait(t_rt)
                VE.do(nc.vector.scalar_tensor_tensor(
                    out_fm[:, oc3, 256 * c:256 * (c + 1)], PS[7][:, 0:256],
                    0.5, out_fm[:, oc3, 256 * c:256 * (c + 1)],
                    op0=OP.mult, op1=OP.add))
                t_psfree[28] = VE.token()
            t_rk_free = VE.token()
            t_rb_free = t_rk_free
        barrier()

        if DEBUG:
            VE.do(nc.vector.tensor_copy(aux[:, 0:2048], out_fm[:, 0, :]))
            t_c = VE.token()
            SY.wait(t_c)
            t_w = SY.dma(dbg["d_inner"], aux[:, 0:2048])
            VE.wait(t_w)
            barrier()

        xt2 = aux[:, 0:1024]
        sq2 = aux[:, 1024:2048]
        ln4 = aux[:, 2048:3072]
        s3t = aux[:, 3072:4096]
        VE.do(nc.vector.memset(stat[:, 40:41], 0.0))
        t_xt_free = None
        t_stats_free = None
        t_ln4_free = None
        t_cp2 = [None, None]
        for tt in range(16):
            SY.wait(t_xt_free)
            t_x = SY.dma(xt2, s2S[tt * 128:(tt + 1) * 128, :])
            t_ln, t_sf = emit_ln(xt2, ln4, 0, sq2, t_x, t_dst_free=t_ln4_free,
                                 t_stats_free=t_stats_free)
            t_stats_free = t_sf
            VE.wait(t_ln)
            VE.do(nc.vector.tensor_tensor(s3t, xt2, ln4, op=OP.add))
            t_s3a = VE.token()
            t_xt_free = [t_s3a, t_sf]
            for och in range(8):
                bk = och % 2
                TE.wait(t_cp2[bk], t_s3a)
                TE.do(nc.tensor.transpose(
                    PS[bk][:, 0:128],
                    out_fm[:, och, tt * 128:(tt + 1) * 128], ident))
                t_tr = TE.token()
                VE.wait(t_tr)
                VE.do(nc.vector.tensor_tensor(
                    s3t[:, och * 128:(och + 1) * 128],
                    s3t[:, och * 128:(och + 1) * 128],
                    PS[bk][:, 0:128], op=OP.add))
                t_cp2[bk] = VE.token()
            t_s3 = VE.token()
            SY.wait(t_s3)
            SY.dma(x3S[tt * 128:(tt + 1) * 128, :], s3t)
            t_ln2_, t_sf2 = emit_ln(s3t, ln4, 0, sq2, t_s3,
                                    t_dst_free=None, t_stats_free=t_sf)
            t_stats_free = t_sf2
            VE.wait(t_ln2_)
            VE.do(nc.vector.reduce_max(stat[:, 42:43], ln4, axis=AX.X))
            VE.fence()
            VE.do(nc.vector.tensor_tensor(stat[:, 40:41], stat[:, 40:41],
                                          stat[:, 42:43], op=OP.max))
            VE.fence()
            VE.do(nc.vector.tensor_scalar(aux[:, 4096:5120], ln4, -1.0,
                                          None, OP.mult))
            VE.do(nc.vector.reduce_max(stat[:, 42:43], aux[:, 4096:5120],
                                       axis=AX.X))
            VE.fence()
            VE.do(nc.vector.tensor_tensor(stat[:, 40:41], stat[:, 40:41],
                                          stat[:, 42:43], op=OP.max))
            VE.fence()
            t_mx2 = VE.token()
            SY.wait(t_ln2_)
            t_lnsp = SY.dma(s1S[tt * 128:(tt + 1) * 128, :], ln4)
            t_ln4_free = [t_lnsp, t_mx2]
        VE.fence()
        t_mxc = VE.token()
        TE.wait(t_mxc, t_psfree.get(0))
        TE.do(nc.tensor.transpose(PS[0][0:1, 0:128], stat[:, 40:41], ident))
        t_tr = TE.token()
        VE.wait(t_tr)
        VE.do(nc.vector.tensor_copy(row1[0:1, 576:704], PS[0][0:1, 0:128]))
        VE.fence()
        t_psfree[0] = VE.token()
        t_lrow = VE.token()
        t_lb = bcast_row(row1[0:1, 576:704], aux[:, 4096:4224], t_lrow, 0, 128)
        VE.wait(t_lb)
        VE.do(nc.vector.reduce_max(stat[:, 44:45], aux[:, 4096:4224],
                                   axis=AX.X))
        VE.fence()
        t_lmx = VE.token()
        SY.wait(t_lmx)
        t_pkd = SY.dma(cc["Mi"][:, 0:1], stat[:, 44:45],
                       allow_slow_non_contiguous=True)
        GE.wait(t_pkd)
        t_cc = collective("M", grp8=True)
        SY.wait(t_cc)
        cc_settle()
        t_gm = SY.dma(aux[:, 4096:4288].rearrange("p (s c) -> p s c", s=8),
                      cc["Mo"].rearrange("(s p) c -> p s c", p=128))
        gm8 = aux[:, 4096:4288].rearrange("p (s c) -> p s c", s=8)[:, :, 0]
        VE.wait(t_gm)
        VE.do(nc.vector.reduce_max(stat[:, 46:47], gm8, axis=AX.X))
        VE.fence()
        VE.do(nc.vector.reciprocal(stat[:, 40:41], stat[:, 46:47]))
        VE.fence()
        VE.do(nc.vector.tensor_scalar(cst[:, 40:41], stat[:, 40:41],
                                      127.0, None, OP.mult))
        VE.fence()
        t_wsb = bcast_cell(cm_cell(9), cst[:, 44:45], None)
        VE.wait(t_wsb)
        VE.fence()
        VE.do(nc.vector.tensor_tensor(stat[:, 42:43], stat[:, 46:47],
                                      cst[:, 44:45], op=OP.mult))
        VE.fence()
        VE.do(nc.vector.tensor_scalar(cst[:, 41:42], stat[:, 42:43],
                                      1.0 / 127.0, None, OP.mult))
        VE.fence()
        t_sc4 = VE.token()
        t_ris = t_sc4
        t_d4 = t_sc4
        wq4 = big2.bitcast(BF16)[:, 0:8192].rearrange("p (d o) -> p d o", d=8)
        for kk in range(8):
            SY.dma(wq4[:, kk, :], wq4S[kk * 128:(kk + 1) * 128, :])
        t_wq4 = SY.dtoken()
        barrier()

        xq4 = big2.bitcast(BF16)[:, 8192:8192 + 16384].rearrange(
            "p (d t) -> p d t", d=8)
        qi4 = aux.bitcast(I32)[:, 4096:5120]
        qb4 = aux.bitcast(BF16)[:, 10240:11264]
        o4f = aux[:, 0:4096].rearrange("p (o t) -> p o t", o=8)
        t_xt_free = None
        t_qb_free = None
        t_cp2 = [None, None]
        for tt in range(16):
            SY.wait(t_xt_free)
            t_x = SY.dma(xt2, s1S[tt * 128:(tt + 1) * 128, :])
            GE.wait(t_x, t_ris, t_qb_free)
            GE.do(nc.gpsimd.tensor_scalar(qi4, xt2, cst[:, 40:41], None,
                                          OP.mult))
            GE.do(nc.gpsimd.tensor_scalar(qb4, qi4, -128.0, 127.0,
                                          OP.max, OP.min))
            t_q = GE.token()
            t_xt_free = t_q
            TE.wait(t_q)
            for ch in range(8):
                bk = ch % 2
                TE.wait(t_cp2[bk])
                TE.do(nc.tensor.transpose(PS[bk].bitcast(BF16)[:, 0:128],
                                          qb4[:, ch * 128:(ch + 1) * 128],
                                          identb))
                t_tr = TE.token()
                VE.wait(t_tr)
                VE.do(nc.vector.tensor_copy(
                    xq4[:, ch, tt * 128:(tt + 1) * 128],
                    PS[bk].bitcast(BF16)[:, 0:128]))
                t_cp2[bk] = VE.token()
            t_qb_free = TE.token()
        barrier()
        t_o4_free = None
        t_ps_free4 = [None, None]
        for sl in range(4):
            t0 = sl * 512
            for och in range(8):
                bq = 5 + (och % 2)
                TE.wait(t_wq4, t_ps_free4[och % 2])
                for dch in range(8):
                    TE.do(nc.tensor.matmul(PS[bq], wq4[:, dch,
                                                       och * 128:(och + 1) * 128],
                                           xq4[:, dch, t0:t0 + 512],
                                           start=(dch == 0), stop=(dch == 7)))
                t_m4 = TE.token()
                VE.wait(t_m4, t_d4, t_o4_free)
                VE.do(nc.vector.tensor_scalar(
                    o4f[:, och, 0:512], PS[bq], cst[:, 41:42],
                    smallc[:, 224 + och:225 + och], OP.mult, OP.add))
                t_ps_free4[och % 2] = VE.token()
            t_d4e = VE.token()
            for ttp in range(4):
                tt = sl * 4 + ttp
                ppx = ttp % 2
                t_x = SY.dma(xbuf[:, ppx, :], x3S[tt * 128:(tt + 1) * 128, :])
                for och in range(8):
                    bk = och % 2
                    TE.wait(t_d4e, t_cp2[bk])
                    TE.do(nc.tensor.transpose(
                        PS[bk][:, 0:128],
                        o4f[:, och, ttp * 128:(ttp + 1) * 128], ident))
                    t_tr = TE.token()
                    VE.wait(t_tr, t_x)
                    VE.do(nc.vector.tensor_tensor(
                        xbuf[:, ppx, och * 128:(och + 1) * 128],
                        xbuf[:, ppx, och * 128:(och + 1) * 128],
                        PS[bk][:, 0:128], op=OP.add))
                    t_cp2[bk] = VE.token()
                t_fin = VE.token()
                SY.wait(t_fin)
                SY.dma(out_hbm[tt * 128:(tt + 1) * 128, :], xbuf[:, ppx, :])
            t_o4_free = TE.token()
        t_done = SY.dtoken()
        SY.wait(t_done)
        GE.wait(t_done)
        TE.wait(t_done)
        barrier()

        if DEBUG:
            for tt in range(16):
                t_l = SY.dma(xbuf[:, 0, :], s1S[tt * 128:(tt + 1) * 128, :])
                SY.wait(t_l)
                t_w = SY.dma(dbg["d_s1"][tt * 128:(tt + 1) * 128, :],
                             xbuf[:, 0, :])
                SY.wait(t_w)
            for tt in range(16):
                t_l = SY.dma(xbuf[:, 0, :], s2S[tt * 128:(tt + 1) * 128, :])
                SY.wait(t_l)
                t_w = SY.dma(dbg["d_s2"][tt * 128:(tt + 1) * 128, :],
                             xbuf[:, 0, :])
                SY.wait(t_w)
            for tt in range(16):
                t_l = SY.dma(xbuf[:, 0, :], x3S[tt * 128:(tt + 1) * 128, :])
                SY.wait(t_l)
                t_w = SY.dma(dbg["d_s3"][tt * 128:(tt + 1) * 128, :],
                             xbuf[:, 0, :])
                SY.wait(t_w)
            for nm, idx in (("d_pj0", 0), ("d_pj16", 16), ("d_pj32", 32),
                            ("d_pj48", 48), ("d_pj64", 64)):
                t_l = SY.dma(xbuf[:, 0, :][:, 0:2048]
                             if False else xbuf.rearrange("p a b -> p (a b)"),
                             pS[idx])
                SY.wait(t_l)
                SY.wait(SY.dma(dbg[nm],
                               xbuf.rearrange("p a b -> p (a b)")))
            SY.wait(SY.dma(dbg["d_st2"][:, 0:48], stat))
            SY.wait(SY.dma(dbg["d_st2"][:, 48:96], cst))
            SY.wait(SY.dma(dbg["d_cum"], hrow2))
            t_l = SY.dma(xbuf[:, 0, :], vT[0:128, :])
            SY.wait(t_l)
            SY.wait(SY.dma(dbg["d_v0"], xbuf[:, 0, :]))
            t_l = SY.dma(xbuf[:, 0, :],
                         kT[0].rearrange("p c j -> p (c j)"))
            SY.wait(t_l)
            SY.wait(SY.dma(dbg["d_k0"], xbuf[:, 0, :]))
            SY.wait(SY.dma(dbg["d_p0"].rearrange("p (c j) -> p c j", c=8),
                           p_fm[:, :, 0:128]))
            SY.wait(SY.dma(dbg["d_ob"], obuf[:, 0, :]))
            SY.wait(SY.dma(dbg["d_nrm"][:, 0:16], recnrm))
            t_l = SY.dma(big2[:, 0:2048], hS[0])
            SY.wait(t_l)
            SY.wait(SY.dma(dbg["d_h0"], big2[:, 0:2048]))
            SY.wait(SY.dma(dbg["d_tm"], tokmax))
            for c2 in range(32):
                t_l = SY.dma(big2[:, 0:2048], hS[c2])
                SY.wait(t_l)
                SY.wait(SY.dma(dbg["d_hall"][c2 * 128:(c2 + 1) * 128, :],
                               big2[:, 0:2048]))
            barrier()

    return nc, dbg


def kernel(**inputs):
    in_maps = stage_inputs(inputs)
    nc, dbg = build_nc()
    res = run_bass_kernel_spmd(nc, in_maps, core_ids=list(range(NCORE)))
    outs = [res.results[c]["out"] for c in range(NCORE)]
    full = np.stack([np.concatenate(outs[0:4], axis=0),
                     np.concatenate(outs[4:8], axis=0)], axis=0)
    if DEBUG:
        kernel._dbg = res.results
    return full
